# version 11
# baseline (speedup 1.0000x reference)
import sys
sys.path.insert(0, "/opt/trn_rl_repo")

import numpy as np
import concourse.bass as bass
import concourse.mybir as mybir
from concourse.bass_utils import run_bass_kernel_spmd




F32 = mybir.dt.float32
BF16 = mybir.dt.bfloat16
AF = mybir.ActivationFunctionType
ALU = mybir.AluOpType
AX = mybir.AxisListType

ENGINES = ("pe", "act", "dve", "pool", "sp")
DMAQ = {"sp": "spq", "pool": "poolq", "act": "actq"}


def _ap_key(ap):
    t = getattr(ap, "tensor", None)
    if t is None:
        return ap.name
    return t.name


def _is_psum(ap):
    t = getattr(ap, "tensor", ap)
    sp = getattr(t, "space", None)
    return sp is not None and str(getattr(sp, "value", sp)) == "psum"


class _Op:
    __slots__ = ("i", "eng", "fn", "reads", "writes", "dma", "tag",
                 "deps", "token", "waits", "vc", "psum_reads", "pre_wait",
                 "phase", "bir", "skip_ldw", "lw_sig", "lw_key")

    def __init__(self, i, eng, fn, reads, writes, dma, tag):
        self.i = i
        self.eng = eng
        self.fn = fn
        self.reads = reads
        self.writes = writes
        self.dma = dma
        self.tag = tag
        self.deps = []
        self.token = None
        self.waits = []
        self.vc = None
        self.skip_ldw = False
        self.lw_sig = None
        self.lw_key = None


def _lw_sig(ap):
    try:
        return (ap.name, ap.offset, tuple(ap.shape), str(ap.stride()),
                ap.base_partition(), str(ap.dtype))
    except Exception:
        return None


class FW:
    def __init__(self, nc):
        self.nc = nc
        self.phase = ""
        self.ops = []
        self._lw = {}
        self._rd = {}
        self.sems = {}
        self.max_tok = {}

    def _rec(self, eng, fn, reads, writes, dma=False, tag=""):
        rl = [a for a in reads if a is not None and not isinstance(a, (int, float))]
        rk = [_ap_key(a) for a in rl]
        wk = [_ap_key(a) for a in writes]
        op = _Op(len(self.ops), eng, fn, rk, wk, dma, tag)
        op.phase = self.phase
        op.psum_reads = [_ap_key(a) for a in rl if _is_psum(a)]
        self.ops.append(op)
        return op

    def mm(self, out, lhsT, rhs, start=True, stop=True, tag=""):
        op = self._rec("pe", lambda e: e.matmul(out, lhsT, rhs, start=start, stop=stop),
                       [lhsT, rhs], [out], tag=tag)
        op.lw_sig = _lw_sig(lhsT)
        op.lw_key = _ap_key(lhsT)

    def trans(self, out, in_, ident, tag=""):
        self._rec("pe", lambda e: e.transpose(out, in_, ident), [in_, ident], [out], tag=tag)

    def act(self, out, in_, func, bias=0.0, scale=1.0, accum_out=None, tag=""):
        def f(e):
            kw = {}
            if accum_out is not None:
                kw["accum_out"] = accum_out
            return e.activation(out, in_, func, bias=bias, scale=scale, **kw)
        wr = [out] if accum_out is None else [out, accum_out]
        self._rec("act", f, [in_, bias if not isinstance(bias, float) else None,
                             scale if not isinstance(scale, float) else None], wr, tag=tag)

    def eng_op(self, eng, fn, reads, writes, tag=""):
        self._rec(eng, fn, reads, writes, tag=tag)

    def tt(self, out, a, b, op, eng="dve", tag=""):
        self._rec(eng, lambda e: e.tensor_tensor(out, a, b, op), [a, b], [out], tag=tag)

    def ts(self, out, in_, s1, s2, op0, op1=ALU.bypass, eng="dve", tag=""):
        self._rec(eng, lambda e: e.tensor_scalar(out, in_, s1, s2, op0, op1),
                  [in_, s1 if not isinstance(s1, (int, float)) else None,
                   s2 if not isinstance(s2, (int, float)) else None], [out], tag=tag)

    def tss(self, out, in_, s, op, eng="dve", tag=""):
        self._rec(eng, lambda e: e.tensor_single_scalar(out, in_, s, op),
                  [in_, s if not isinstance(s, (int, float)) else None], [out], tag=tag)

    def stt(self, out, in0, s, in1, op0, op1, eng="dve", tag=""):
        self._rec(eng, lambda e: e.scalar_tensor_tensor(out, in0, s, in1, op0, op1),
                  [in0, in1, s if not isinstance(s, (int, float)) else None], [out], tag=tag)

    def red(self, out, in_, op, negate=False, eng="dve", tag=""):
        self._rec(eng, lambda e: e.tensor_reduce(out, in_, AX.X, op, negate=negate),
                  [in_], [out], tag=tag)

    def copy(self, out, in_, eng="dve", tag=""):
        self._rec(eng, lambda e: e.tensor_copy(out, in_), [in_], [out], tag=tag)

    def recip(self, out, in_, tag=""):
        self._rec("dve", lambda e: e.reciprocal(out, in_), [in_], [out], tag=tag)

    def memset(self, ap, v, eng="dve", tag=""):
        self._rec(eng, lambda e: e.memset(ap, v), [], [ap], tag=tag)

    def dma(self, out, in_, eng="sp", tag=""):
        self._rec(eng, lambda e: e.dma_start(out, in_), [in_], [out], dma=True, tag=tag)

    def barrier_note(self):
        pass

    def finalize(self, stack):
        ops = self.ops
        lw = {}
        rd = {}
        psum_rd = {}
        pe_sig = pe_ver = None
        for op in ops:
            dep = set()
            for k in op.reads:
                if k in lw:
                    dep.add(lw[k])
            for k in op.psum_reads:
                for (r, e) in psum_rd.get(k, ()):
                    if e != op.eng:
                        dep.add(r)
            for k in op.writes:
                if k in lw:
                    dep.add(lw[k])
                for r in rd.get(k, ()):
                    dep.add(r)
            dep.discard(op.i)
            last_per_eng = {}
            dma_deps = set()
            for d in dep:
                dop = ops[d]
                if dop.dma:
                    dma_deps.add(d)
                    continue
                if dop.eng == "pe" and op.eng == "pe" and not op.dma:
                    continue
                if d > last_per_eng.get(dop.eng, -1):
                    last_per_eng[dop.eng] = d
            op.deps = sorted(dma_deps | set(last_per_eng.values()))
            if op.eng == "pe":
                if op.lw_sig is not None:
                    ver = lw.get(op.lw_key, -1)
                    if pe_sig is not None and op.lw_sig == pe_sig and ver == pe_ver:
                        op.skip_ldw = True
                    pe_sig, pe_ver = op.lw_sig, ver
                else:
                    pe_sig = pe_ver = None
            for k in op.reads:
                rd.setdefault(k, []).append(op.i)
            for k in op.psum_reads:
                psum_rd.setdefault(k, []).append((op.i, op.eng))
            for k in op.writes:
                lw[k] = op.i
                rd[k] = []
                psum_rd[k] = []

        needed = set()
        for op in ops:
            for d in op.deps:
                needed.add(d)
        NSEM = 16
        counters = {}
        qcount = {}
        for op in ops:
            if op.dma:
                q = DMAQ[op.eng]
                i = qcount.get(q, 0)
                qcount[q] = i + 1
                stream = f"{q}{i % NSEM}"
                epoch = i // NSEM
                op.token = (stream, 16 * (epoch + 1))
                counters[stream] = 16 * (epoch + 1)
                op.pre_wait = (stream, 16 * epoch) if epoch > 0 else None
            else:
                op.pre_wait = None
                if op.i in needed:
                    stream = op.eng
                    counters[stream] = counters.get(stream, 0) + 1
                    op.token = (stream, counters[stream])
        self.max_tok = dict(counters)

        evc = {e: {} for e in ENGINES}
        svc = {e: {} for e in ENGINES}
        for op in ops:
            know = evc[op.eng]
            waits = []
            mvc = dict(svc[op.eng])
            for d in op.deps:
                dop = ops[d]
                if dop.token is None:
                    continue
                s, v = dop.token
                if know.get(s, 0) < v:
                    waits.append((s, v))
                    know[s] = v
                    for ks, kv in dop.vc.items():
                        if know.get(ks, 0) < kv:
                            know[ks] = kv
                for ks, kv in dop.vc.items():
                    if mvc.get(ks, 0) < kv:
                        mvc[ks] = kv
            if op.pre_wait is not None:
                s, v = op.pre_wait
                if know.get(s, 0) < v:
                    waits.append((s, v))
                    know[s] = v
            wmax = {}
            for s, v in waits:
                wmax[s] = max(wmax.get(s, 0), v)
            op.waits = sorted(wmax.items())
            svc_update = dict(mvc)
            if op.token is not None:
                s, v = op.token
                mvc = dict(mvc)
                mvc[s] = v
                if not op.dma:
                    svc_update[s] = v
            op.vc = mvc
            svc[op.eng] = svc_update

        streams = set()
        for op in ops:
            if op.token:
                streams.add(op.token[0])
        for s in sorted(streams):
            self.sems[s] = stack.enter_context(self.nc.semaphore(f"s_{s}"))

        return self

    def emit(self, final_waits=True):
        nc = self.nc
        sems = self.sems
        by_eng = {e: [] for e in ENGINES}
        for op in self.ops:
            by_eng[op.eng].append(op)
        self.flip_names = []

        def body(eng_name):
            def run(e):
                for op in by_eng[eng_name]:
                    for s, v in op.waits:
                        e.wait_ge(sems[s], v)
                    inst = op.fn(e)
                    op.bir = inst.ins.name
                    if op.skip_ldw:
                        self.flip_names.append(op.bir)
                    if op.token is not None:
                        s, v = op.token
                        inst.then_inc(sems[s], 16 if op.dma else 1)
                if eng_name == "sp" and final_waits:
                    for s, v in self.max_tok.items():
                        if s in sems:
                            e.wait_ge(sems[s], v)
            return run

        with nc.Block() as block:
            block.tensor(body("pe"))
            block.scalar(body("act"))
            block.vector(body("dve"))
            block.gpsimd(body("pool"))
            block.sync(body("sp"))

    def stats(self):
        n = {e: 0 for e in ENGINES}
        w = 0
        t = 0
        for op in self.ops:
            n[op.eng] += 1
            w += len(op.waits)
            t += 1 if op.token else 0
        return {"ops": len(self.ops), "per_engine": n, "waits": w, "tokens": t}





class Ctx:
    pass


NPOOL = 18


def alloc_common(nc, stack, fw):
    C = Ctx()
    sb = lambda name, shape, dt: stack.enter_context(nc.sbuf_tensor(name, shape, dt))
    ps = lambda name, shape, dt: stack.enter_context(nc.psum_tensor(name, shape, dt))

    C.xT = [sb(f"xT{c}", [128, 2048], F32) for c in range(8)]
    C.hT = [sb(f"hT{c}", [128, 2048], BF16) for c in range(8)]
    C.pool = [sb(f"pl{i}", [128, 2048], BF16) for i in range(NPOOL)]

    C.AL = ps("psAL", [128, 1024], F32)
    C.AH = ps("psAH", [128, 1024], F32)
    C.BL = ps("psBL", [128, 1024], F32)
    C.BH = ps("psBH", [128, 1024], F32)
    C.An = [C.AL[:, 0:512], C.AL[:, 512:1024], C.AH[:, 0:512], C.AH[:, 512:1024]]
    C.Bn = [C.BL[:, 0:512], C.BL[:, 512:1024], C.BH[:, 0:512], C.BH[:, 512:1024]]
    C.Ah = [C.AL, C.AH]
    C.Bh = [C.BL, C.BH]
    C.PTbB = C.BL.bitcast(BF16)
    C.PTb4 = [C.AL.bitcast(BF16), C.AH.bitcast(BF16),
              C.BL.bitcast(BF16), C.BH.bitcast(BF16)]
    C.grp = 0

    C.mean = sb("ln_mean", [128, 2048], F32)
    C.var = sb("ln_var", [128, 2048], F32)
    C.xb = [sb(f"ln_xb{i}", [128, 2048], BF16) for i in range(2)]

    C.scores_sb = C.mean[0:32, :]
    C.combT = C.var[0:32, :]
    C.row_f32 = C.var[64:65, :]
    C.row_bf = C.mean.bitcast(BF16)[0:1, 0:2048]

    C.allones = sb("allones", [128, 128], BF16)
    C.ones1 = sb("ones1", [1, 128], BF16)
    C.eye_bf = sb("eye_bf", [128, 128], BF16)
    C.eye_f32 = sb("eye_f32", [128, 128], F32)
    C.cst = sb("cst", [128, 8], F32)
    C.EPS, C.NEGHALF, C.QSCALE, C.SCALE25, C.EPS20, C.INV32 = range(6)
    return C


def cst(C, i):
    return C.cst[:, i:i + 1]


def emit_const_init(fw, C, dram):
    fw.dma(C.allones[:, :], dram["c_allones"], eng="pool")
    fw.dma(C.ones1[:, :], dram["c_ones1"], eng="pool")
    fw.dma(C.eye_bf[:, :], dram["c_eye"], eng="pool")
    fw.dma(C.eye_f32[:, :], dram["c_eye"], eng="sp")
    fw.dma(C.cst[:, :], dram["c_cst"], eng="sp")


def load_xT(fw, C, x_dram, s):
    src = x_dram.rearrange("b (c p) t -> b c p t", p=128)
    for c in range(8):
        fw.dma(C.xT[c][:, :], src[s, c, :, :], eng="sp", tag=f"ldx{c}")


def store_xT(fw, C, y_dram, s):
    dst = y_dram.rearrange("b (c p) t -> b c p t", p=128)
    for c in range(8):
        fw.dma(dst[s, c, :, :], C.xT[c][:, :], eng="sp", tag=f"stx{c}")


def _halves(full):
    return (full[:, 0:1024], full[:, 1024:2048])


def layernorm(fw, C, tag=""):
    fw.phase = tag + "LN"
    """hT = LayerNorm(xT). Stats via all-ones matmul (sums broadcast on all
    partitions). h computed as x*rstd - (mean*rstd)."""
    for c in range(8):
        xb = C.xb[c % 2]
        fw.act(xb[:, :], C.xT[c][:, :], AF.Copy, tag=f"{tag}lnc{c}")
        for n in range(4):
            sl = slice(n * 512, (n + 1) * 512)
            fw.mm(C.An[n], C.allones[:, :], xb[:, sl], start=(c == 0), stop=(c == 7))
    for h in range(2):
        fw.ts(C.mean[:, h * 1024:(h + 1) * 1024], C.Ah[h][:, :], 1.0 / 1024.0, None,
              ALU.mult, tag=f"{tag}lnmean{h}")
    for c in range(8):
        xq = C.xb[c % 2]
        fw.act(xq[:, :], C.xT[c][:, :], AF.Square, tag=f"{tag}lnsq{c}")
        for n in range(4):
            sl = slice(n * 512, (n + 1) * 512)
            fw.mm(C.Bn[n], C.allones[:, :], xq[:, sl], start=(c == 0), stop=(c == 7))
    for h in range(2):
        hsl = slice(h * 1024, (h + 1) * 1024)
        fw.act(C.var[:, hsl], C.mean[:, hsl], AF.Square, tag=f"{tag}lnmsq{h}")
        fw.stt(C.var[:, hsl], C.Bh[h][:, :], 1.0 / 1024.0, C.var[:, hsl],
               ALU.mult, ALU.subtract)
        fw.act(C.var[:, hsl], C.var[:, hsl], AF.Ln, bias=cst(C, C.EPS))
        fw.act(C.var[:, hsl], C.var[:, hsl], AF.Exp, scale=cst(C, C.NEGHALF))
        fw.tt(C.mean[:, hsl], C.mean[:, hsl], C.var[:, hsl], ALU.mult)
    for c in range(8):
        eng = "dve" if c % 2 == 0 else "pool"
        fw.tt(C.hT[c][:, :], C.xT[c][:, :], C.var[:, :], ALU.mult, eng=eng)
        fw.tt(C.hT[c][:, :], C.hT[c][:, :], C.mean[:, :], ALU.subtract, eng=eng,
              tag=f"{tag}lnh{c}")


def _tc(fw, C, dst, in_, ident, k, rows, cols, rot=(0, 1, 2, 3), tag=""):
    reg = C.PTb4[rot[k % len(rot)]][0:cols, 0:rows]
    fw.trans(reg, in_, ident)
    if k % 2 == 0:
        fw.copy(dst, reg, tag=tag)
    else:
        fw.act(dst, reg, AF.Copy, tag=tag)


def _stream_w_chunk(fw, C, dst_ap, w_dram, col0, ncol, tag=""):
    src = w_dram.rearrange("(a p) n -> p a n", p=128)
    fw.dma(dst_ap.rearrange("p (a n) -> p a n", n=ncol),
           src[:, :, col0:col0 + ncol], eng="pool", tag=tag)


def _proj_1024(fw, C, slot, dest_half_ops, rhs_chunks):
    Pn, Ph = (C.An, C.Ah) if C.grp == 0 else (C.Bn, C.Bh)
    C.grp ^= 1
    for kc in range(8):
        lhsT = slot[:, kc * 128:(kc + 1) * 128]
        for n in range(4):
            sl = slice(n * 512, (n + 1) * 512)
            fw.mm(Pn[n], lhsT, rhs_chunks[kc][:, sl], start=(kc == 0), stop=(kc == 7))
    for h in range(2):
        dest_half_ops(h, Ph[h])



def alloc_l0_attn(nc, stack, fw, C):
    sb = lambda name, shape, dt: stack.enter_context(nc.sbuf_tensor(name, shape, dt))
    P = C.pool
    C.qb, C.kb, C.vb = P[0], P[1], P[2]
    C.pTx = [P[3], P[12]]
    C.svT = P[4:12]
    C.wsQ = P[13][:, 0:1024]
    C.wsK = P[13][:, 1024:2048]
    C.wsV = P[14][:, 0:1024]
    C.wsO = P[14][:, 1024:2048]
    C.vtm = [sb("vtm0", [128, 16 * 65], BF16),
             C.xb[0][:, 0:16 * 65]]
    C.bias2 = sb("bias2", [128, 32], F32)


def l0_attn_init(fw, C, W):
    fw.dma(C.bias2[:, :], W["b_l0"], eng="sp")
    fw.ts(C.bias2[:, 0:8], C.bias2[:, 0:8], 0.125, None, ALU.mult)


def _vtm_ones(fw, C, h):
    ap = C.vtm[h].rearrange("p (a n) -> p a n", n=65)
    fw.memset(ap[:, :, 64:65], 1.0)


def l0_attn(fw, C, W, seq_tag=""):
    layernorm(fw, C, tag=seq_tag + "L0a.")

    for pair in range(8):
        fw.phase = seq_tag + "l0.qkv"
        for wname, slot in (("wqT", C.wsQ), ("wkT", C.wsK), ("wvT", C.wsV)):
            _stream_w_chunk(fw, C, slot, W[wname], pair * 128, 128,
                            tag=f"{seq_tag}w{wname}{pair}")
        for (slot, dst, bofs, scale_ap) in (
            (C.wsQ, C.qb, 0, cst(C, C.QSCALE)),
            (C.wsK, C.kb, 8, None),
            (C.wsV, C.vb, 16, None),
        ):
            def evac(h, ph, dst=dst, bofs=bofs, scale_ap=scale_ap):
                kw = {"bias": C.bias2[:, bofs + pair:bofs + pair + 1]}
                if scale_ap is not None:
                    kw["scale"] = scale_ap
                fw.act(_halves(dst)[h], ph[:, :], AF.Identity,
                       tag=f"{seq_tag}ev{bofs}.{pair}", **kw)
            _proj_1024(fw, C, slot, evac, C.hT)

        for h in (0, 1):
            fw.phase = seq_tag + "l0.head"
            hs = slice(64 * h, 64 * h + 64)
            vtm = C.vtm[h]
            ident = C.eye_bf[hs, hs]
            _vtm_ones(fw, C, h)
            for kc in range(16):
                _tc(fw, C, vtm[:, kc * 65:kc * 65 + 64],
                    C.vb[hs, kc * 128:(kc + 1) * 128], ident, kc, 64, 128,
                    tag=f"{seq_tag}vtm")
            fw.phase = seq_tag + "l0.sc"
            sv = C.svT[pair]
            for qh in range(2):
                pvh = C.Bh[qh]
                pvn = (C.Bn[2 * qh], C.Bn[2 * qh + 1])
                for kc in range(17):
                    if kc < 16:
                        lhsT = C.kb[hs, kc * 128:(kc + 1) * 128]
                        sb_ = C.Ah[kc % 2]
                        pt = C.pTx[kc % 2]
                        for n in range(2):
                            fw.mm(sb_[:, n * 512:(n + 1) * 512], lhsT,
                                  C.qb[hs, qh * 1024 + n * 512:
                                       qh * 1024 + (n + 1) * 512],
                                  start=True, stop=True)
                        fw.act(pt[:, 0:1024], sb_[:, :], AF.Exp,
                               tag=f"{seq_tag}exp{pair}.{h}.{qh}.{kc}")
                    if kc >= 1:
                        j = kc - 1
                        ptj = C.pTx[j % 2]
                        for n in range(2):
                            fw.mm(pvn[n][0:65, :], vtm[:, j * 65:(j + 1) * 65],
                                  ptj[:, n * 512:(n + 1) * 512],
                                  start=(j == 0), stop=(j == 15))
                hsl = slice(qh * 1024, (qh + 1) * 1024)
                fw.recip(C.row_f32[0:1, hsl], pvh[64:65, :])
                fw.act(C.row_bf[0:1, hsl], C.row_f32[0:1, hsl], AF.Copy)
                fw.act(sv[hs, hsl], pvh[0:64, :], AF.Copy,
                       tag=f"{seq_tag}svev{pair}.{h}.{qh}")
                for n in range(2):
                    fw.mm(pvn[n], C.ones1[0:1, :],
                          C.row_bf[0:1, qh * 1024 + n * 512:qh * 1024 + (n + 1) * 512],
                          start=True, stop=True)
                fw.tt(sv[hs, hsl], sv[hs, hsl], pvh[0:64, :], ALU.mult,
                      tag=f"{seq_tag}svn{pair}.{h}.{qh}")

    for o in range(8):
        fw.phase = seq_tag + "l0.oproj"
        slot = C.wsQ if o % 2 == 0 else C.wsO
        _stream_w_chunk(fw, C, slot, W["woT"], o * 128, 128, tag=f"{seq_tag}wo{o}")

        def evac(h, ph, o=o):
            hsl = slice(h * 1024, (h + 1) * 1024)
            fw.stt(C.xT[o][:, hsl], ph[:, :], C.bias2[:, 24 + o:24 + o + 1],
                   C.xT[o][:, hsl], ALU.add, ALU.add, tag=f"{seq_tag}xadd{o}.{h}")
        _proj_1024(fw, C, slot, evac, C.svT)



def alloc_l0_moe(nc, stack, fw, C):
    sb = lambda name, shape, dt: stack.enter_context(nc.sbuf_tensor(name, shape, dt))
    P = C.pool
    C.guc = [[P[e * 4 + ic] for ic in range(4)] for e in range(2)]
    C.c_sb = [P[8], P[9]]
    C.sg = P[10]
    C.wg_ic = [P[11], P[12]]
    C.wu_ic = [P[13], P[14]]
    C.wd_sl = P[15]
    C.gwT = sb("gw_sb", [128, 256], BF16)
    C.gb_sb = sb("gb_sb", [128, 32], F32)
    C.scores_tm = sb("scores_tm", [128, 512], F32)
    C.sc_tm = sb("sc_tm", [128, 512], F32)
    C.comb_tm = sb("comb_tm", [128, 512], F32)
    P0f = P[0].bitcast(F32)
    P1f = P[1].bitcast(F32)
    P2f = P[2].bitcast(F32)
    C.r_msk = P0f[:, 0:512]
    C.r_m8 = P0f[:, 512:1024]
    C.r_eq = P1f[:, 0:512]
    C.r_num = P1f[:, 512:1024]
    C.r_gs = P2f[:, 0:128]
    C.r_t1 = P2f[:, 128:256]
    C.r_gm = P2f[:, 256:384]
    C.r_mx = P2f[:, 384:400]
    C.r_den = P2f[:, 400:416]
    C.r_h = (P0f, P1f, P2f)


def l0_moe_init(fw, C, W):
    src = W["gwT"].rearrange("(a p) n -> p a n", p=128)
    fw.dma(C.gwT.ap().rearrange("p (a n) -> p a n", n=32), src, eng="pool")
    fw.dma(C.gb_sb[:, :], W["gb128"], eng="sp")


def _routing(fw, C, seq_tag=""):
    fw.phase = seq_tag + "routing"
    """DeepSeek router: sigmoid scores, group top-2-sum, top-4 groups,
    top-8 experts, normalized+scaled combine. All token chunks batched:
    [128 tokens-per-chunk, 16 chunks, ...] on the free dim."""
    import concourse.bass as _bass
    for kc in range(8):
        lhsT = C.gwT[:, kc * 32:(kc + 1) * 32]
        for n in range(4):
            sl = slice(n * 512, (n + 1) * 512)
            fw.mm(C.An[n][0:32, :], lhsT, C.hT[kc][:, sl], start=(kc == 0),
                  stop=(kc == 7))
    for h in range(2):
        fw.act(C.scores_sb[0:32, h * 1024:(h + 1) * 1024], C.Ah[h][0:32, :],
               AF.Sigmoid, tag=f"{seq_tag}sig{h}")
    for tc in range(16):
        ts32 = slice(tc * 32, (tc + 1) * 32)
        bb = C.Bh[tc % 2]
        fw.trans(bb[:, 0:32], C.scores_sb[0:32, tc * 128:(tc + 1) * 128],
                 C.eye_f32[0:32, 0:32])
        if tc % 2 == 0:
            fw.copy(C.scores_tm[:, ts32], bb[:, 0:32], tag=f"{seq_tag}sctm{tc}")
        else:
            fw.act(C.scores_tm[:, ts32], bb[:, 0:32], AF.Copy,
                   tag=f"{seq_tag}sctm{tc}")
    def v3(t, e):
        return t.rearrange("p (t e) -> p t e", e=e)
    gbb = _bass.AP(C.gb_sb, 0, [[32, 128], [0, 16], [1, 32]])
    fw.eng_op("dve", lambda eng: eng.tensor_tensor(
        v3(C.scores_tm[:, :], 32), v3(C.scores_tm[:, :], 32), gbb, ALU.add),
        [C.scores_tm[:, :], C.gb_sb[:, :]], [C.sc_tm[:, :]], tag=f"{seq_tag}scb")
    fw.ops.pop()
    fw.eng_op("dve", lambda eng: eng.tensor_tensor(
        v3(C.sc_tm[:, :], 32), v3(C.scores_tm[:, :], 32), gbb, ALU.add),
        [C.scores_tm[:, :], C.gb_sb[:, :]], [C.sc_tm[:, :]], tag=f"{seq_tag}scb")
    sc4 = C.sc_tm[:, :].rearrange("p (t g e) -> p t g e", g=8, e=4)
    gs3 = v3(C.r_gs, 8)
    t13 = v3(C.r_t1, 8)
    first = True
    for i in range(4):
        for j in range(i + 1, 4):
            fw.tt(t13, sc4[:, :, :, i], sc4[:, :, :, j], ALU.add)
            if first:
                fw.copy(C.r_gs, C.r_t1)
                first = False
            else:
                fw.tt(gs3, gs3, t13, ALU.max)
    fw.memset(C.r_gm, 0.0)
    mxb8 = _bass.AP(C.r_mx.tensor, C.r_mx.offset, [[1024, 128], [1, 16], [0, 8]])
    gm3 = v3(C.r_gm, 8)
    eq3 = v3(C.r_eq[:, 0:128], 8)
    for it in range(4):
        fw.red(C.r_mx, gs3, ALU.max)
        fw.tt(eq3, gs3, mxb8, ALU.is_equal)
        fw.tt(gm3, gm3, eq3, ALU.add)
        fw.stt(C.r_gs, C.r_eq[:, 0:128], -2e9, C.r_gs, ALU.mult, ALU.add)
    msk4 = C.r_msk.rearrange("p (t g e) -> p t g e", g=8, e=4)
    gmb4 = _bass.AP(C.r_gm.tensor, C.r_gm.offset, [[1024, 128], [8, 16], [1, 8], [0, 4]])
    fw.eng_op("dve", lambda eng: eng.tensor_tensor(msk4, sc4, gmb4, ALU.mult),
              [C.sc_tm[:, :], C.r_gm], [C.r_msk], tag=f"{seq_tag}msk")
    fw.memset(C.r_m8, 0.0)
    msk3 = v3(C.r_msk, 32)
    m83 = v3(C.r_m8, 32)
    eq32 = v3(C.r_eq, 32)
    mxb32 = _bass.AP(C.r_mx.tensor, C.r_mx.offset, [[1024, 128], [1, 16], [0, 32]])
    for it in range(8):
        fw.red(C.r_mx, msk3, ALU.max)
        fw.tt(eq32, msk3, mxb32, ALU.is_equal)
        fw.tt(m83, m83, eq32, ALU.add)
        fw.stt(C.r_msk, C.r_eq, -2e9, C.r_msk, ALU.mult, ALU.add)
    fw.tt(C.r_num, C.r_m8, C.scores_tm[:, :], ALU.mult)
    fw.red(C.r_den, v3(C.r_num, 32), ALU.add)
    fw.ts(C.r_den, C.r_den, 1e-20, None, ALU.add)
    fw.recip(C.r_den, C.r_den)
    fw.ts(C.r_den, C.r_den, 2.5, None, ALU.mult)
    denb = _bass.AP(C.r_den.tensor, C.r_den.offset, [[1024, 128], [1, 16], [0, 32]])
    fw.eng_op("dve", lambda eng: eng.tensor_tensor(
        v3(C.comb_tm[:, :], 32), v3(C.r_num, 32), denb, ALU.mult),
        [C.r_num, C.r_den], [C.comb_tm[:, :]], tag=f"{seq_tag}comb")
    for tc in range(16):
        ts32 = slice(tc * 32, (tc + 1) * 32)
        bb = C.Bh[tc % 2]
        fw.trans(bb[0:32, 0:128], C.comb_tm[:, ts32], C.eye_f32[:, :])
        if tc % 2 == 0:
            fw.copy(C.combT[0:32, tc * 128:(tc + 1) * 128], bb[0:32, 0:128],
                    tag=f"{seq_tag}cT{tc}")
        else:
            fw.act(C.combT[0:32, tc * 128:(tc + 1) * 128], bb[0:32, 0:128],
                   AF.Copy, tag=f"{seq_tag}cT{tc}")


def l0_moe(fw, C, W, seq_tag=""):
    layernorm(fw, C, tag=seq_tag + "moe.")
    _routing(fw, C, seq_tag=seq_tag)

    def dma_gu(which, par, src, ic, tag):
        slot = (C.wg_ic if which == "g" else C.wu_ic)[par]
        dst = slot[:, (ic % 2) * 1024:(ic % 2 + 1) * 1024]
        sr = src.rearrange("(a p) n -> p a n", p=128)
        fw.dma(dst.rearrange("p (a n) -> p a n", n=128),
               sr[:, :, ic * 128:(ic + 1) * 128], eng="pool", tag=tag)

    for pair in range(17):
        shared = (pair == 16)
        es = (2 * pair, 2 * pair + 1) if not shared else (0, 1)
        srcs = []
        for par, e in enumerate(es):
            if shared:
                srcs.append((W["shg2"][e], W["shu2"][e], W["shd2"][e]))
            else:
                srcs.append((W["wgT"][e], W["wuT"][e], W["wdT"][e]))
        for par, e in enumerate(es):
            for ic in (0, 1):
                dma_gu("g", par, srcs[par][0], ic, f"{seq_tag}wg{e}.{ic}")
                dma_gu("u", par, srcs[par][1], ic, f"{seq_tag}wu{e}.{ic}")
        def emit_csb():
            if shared:
                return
            for par2, e2 in enumerate(es):
                fw.dma(C.row_f32, C.combT[e2:e2 + 1, :], eng="act",
                       tag=f"{seq_tag}crow{e2}")
                fw.act(C.row_bf, C.row_f32, AF.Copy)
                for n in range(4):
                    sl = slice(n * 512, (n + 1) * 512)
                    fw.mm(C.Bn[n], C.ones1[0:1, :], C.row_bf[0:1, sl],
                          start=True, stop=True)
                for h in range(2):
                    fw.act(_halves(C.c_sb[par2])[h], C.Bh[h][:, :], AF.Copy,
                           tag=f"{seq_tag}csb{e2}.{h}")
        for par, e in enumerate(es):
            fw.phase = seq_tag + "moe.gu"
            for ic in range(4):
                gslot = C.wg_ic[par][:, (ic % 2) * 1024:(ic % 2 + 1) * 1024]
                uslot = C.wu_ic[par][:, (ic % 2) * 1024:(ic % 2 + 1) * 1024]
                for kc in range(8):
                    lhsT = gslot[:, kc * 128:(kc + 1) * 128]
                    for n in range(4):
                        sl = slice(n * 512, (n + 1) * 512)
                        fw.mm(C.An[n], lhsT, C.hT[kc][:, sl], start=(kc == 0),
                              stop=(kc == 7))
                if ic < 2:
                    dma_gu("g", par, srcs[par][0], ic + 2, f"{seq_tag}wg{e}.{ic+2}")
                if par == 0 and ic == 0:
                    emit_csb()
                for h in range(2):
                    fw.act(_halves(C.sg)[h], C.Ah[h][:, :], AF.Silu,
                           tag=f"{seq_tag}silu{e}.{ic}.{h}")
                for kc in range(8):
                    lhsT = uslot[:, kc * 128:(kc + 1) * 128]
                    for n in range(4):
                        sl = slice(n * 512, (n + 1) * 512)
                        fw.mm(C.Bn[n], lhsT, C.hT[kc][:, sl], start=(kc == 0),
                              stop=(kc == 7))
                if ic < 2:
                    dma_gu("u", par, srcs[par][1], ic + 2, f"{seq_tag}wu{e}.{ic+2}")
                for h in range(2):
                    gh = _halves(C.guc[par][ic])[h]
                    fw.act(gh, C.Bh[h][:, :], AF.Copy, tag=f"{seq_tag}uev{e}.{ic}.{h}")
                    fw.tt(gh, gh, _halves(C.sg)[h], ALU.mult, eng="pool")
                    if not shared:
                        fw.tt(gh, gh, _halves(C.c_sb[par])[h], ALU.mult, eng="pool",
                              tag=f"{seq_tag}guc{e}.{ic}.{h}")
        fw.phase = seq_tag + "moe.down"

        def dma_wd(o2):
            half_ = C.wd_sl[:, (o2 % 2) * 1024:(o2 % 2 + 1) * 1024]
            for par2, e2 in enumerate(es):
                dsr = srcs[par2][2][:, o2 * 128:(o2 + 1) * 128].rearrange(
                    "(a p) n -> p a n", p=128)
                fw.dma(half_[:, par2 * 512:(par2 + 1) * 512].rearrange(
                    "p (a n) -> p a n", n=128), dsr,
                    eng="pool", tag=f"{seq_tag}wd{e2}.{o2}")
        dma_wd(0)
        dma_wd(1)
        for o in range(8):
            half = C.wd_sl[:, (o % 2) * 1024:(o % 2 + 1) * 1024]
            Pn, Ph = (C.An, C.Ah) if o % 2 == 0 else (C.Bn, C.Bh)
            for par in range(2):
                for ic in range(4):
                    lhsT = half[:, par * 512 + ic * 128:par * 512 + (ic + 1) * 128]
                    for n in range(4):
                        sl = slice(n * 512, (n + 1) * 512)
                        fw.mm(Pn[n], lhsT, C.guc[par][ic][:, sl],
                              start=(par == 0 and ic == 0), stop=(par == 1 and ic == 3))
            if o < 6:
                dma_wd(o + 2)
            for h in range(2):
                hsl = slice(h * 1024, (h + 1) * 1024)
                fw.tt(C.xT[o][:, hsl], C.xT[o][:, hsl], Ph[h][:, :], ALU.add,
                      tag=f"{seq_tag}moex{o}.{h}")



def alloc_l1_attn(nc, stack, fw, C):
    sb = lambda name, shape, dt: stack.enter_context(nc.sbuf_tensor(name, shape, dt))
    P = C.pool
    C.n_k1 = [P[3][:, 0:1024], P[3][:, 1024:2048]]
    C.n_k3T = [P[15][:, 0:1024], P[15][:, 1024:2048]]
    C.n_k3 = P[16][0:64, :]
    C.n_k1T = P[17][0:64, :]
    C.n_lmf = [sb(f"n_lmf{i}", [128, 128], F32) for i in range(2)]
    C.n_lmb = [sb(f"n_lmb{i}", [128, 128], BF16) for i in range(2)]
    C.n_s = [sb(f"n_s{i}", [64, 11 * 64], BF16) for i in range(2)]
    C.n_cI = sb("n_cI", [64, 192], BF16)
    C.n_k1sum = [sb(f"n_k1sum{i}", [128, 16], F32) for i in range(2)]
    C.n_vsmall = [sb(f"n_vsmall{i}", [128, 4], F32) for i in range(2)]


def _sl64(t, i):
    return t[:, i * 64:(i + 1) * 64]


def l1_attn_init(fw, C):
    for i, v in enumerate((7.0, 15.0, 13.0)):
        fw.ts(_sl64(C.n_cI, i), C.eye_bf[0:64, 0:64], v, None, ALU.mult)


def l1_attn(fw, C, W, seq_tag=""):
    layernorm(fw, C, tag=seq_tag + "L1a.")

    for pair in range(8):
        fw.phase = seq_tag + "l1.qkv"
        for col0, slot in ((0, C.wsQ), (1024, C.wsK), (2048, C.wsV)):
            _stream_w_chunk(fw, C, slot, W["w_qkv"], col0 + pair * 128, 128,
                            tag=f"{seq_tag}l1w{pair}.{col0}")
        for (slot, dst, scale_ap) in ((C.wsQ, C.qb, cst(C, C.QSCALE)),
                                      (C.wsK, C.kb, None), (C.wsV, C.vb, None)):
            def evac(h, ph, dst=dst, scale_ap=scale_ap):
                if scale_ap is not None:
                    fw.act(_halves(dst)[h], ph[:, :], AF.Identity,
                           scale=scale_ap, tag=f"{seq_tag}l1ev{pair}.{h}")
                else:
                    fw.act(_halves(dst)[h], ph[:, :], AF.Copy,
                           tag=f"{seq_tag}l1ev{pair}.{h}")
            _proj_1024(fw, C, slot, evac, C.hT)

        fw.phase = seq_tag + "l1.lm"
        for h in (0, 1):
            hs = slice(64 * h, 64 * h + 64)
            fw.red(C.n_lmf[h][hs, 0:64],
                   C.kb[hs, :].rearrange("p (m w) -> p m w", w=32), ALU.add)
            fw.act(C.n_lmb[h][hs, 0:64], C.n_lmf[h][hs, 0:64], AF.Identity,
                   scale=cst(C, C.INV32)[hs, :])
            fw.red(C.n_lmf[h][hs, 64:128],
                   C.qb[hs, :].rearrange("p (m w) -> p m w", w=32), ALU.add)
            fw.act(C.n_lmb[h][hs, 64:128], C.n_lmf[h][hs, 64:128], AF.Identity,
                   scale=cst(C, C.INV32)[hs, :])
        for h in (0, 1):
            fw.phase = seq_tag + "l1.head"
            hs = slice(64 * h, 64 * h + 64)
            vtm = C.vtm[h]
            S = C.n_s[h]
            K2T, KVb, KVT, Aw, Cw, Ew, Vb, VT, M1b, M2b, K2b = range(11)
            ident = C.eye_bf[hs, hs]
            ident0 = C.eye_bf[0:64, 0:64]
            k1 = C.n_k1[h]
            k3 = C.n_k3
            k3T = C.n_k3T[h]
            k1T = C.n_k1T
            vsm = C.n_vsmall[h]
            _vtm_ones(fw, C, h)
            for kc in range(16):
                _tc(fw, C, vtm[:, kc * 65:kc * 65 + 64],
                    C.vb[hs, kc * 128:(kc + 1) * 128], ident, kc, 64, 128,
                    tag=f"{seq_tag}nvtm")
            klm = C.n_lmb[h][hs, 0:64]
            qlm = C.n_lmb[h][hs, 64:128]
            for tc in range(16):
                ah = C.Ah[tc // 8]
                ofs = (tc % 8) * 128
                fw.mm(ah[:, ofs:ofs + 64], C.qb[hs, tc * 128:(tc + 1) * 128],
                      klm, start=True, stop=True)
            for tc in range(16):
                ah = C.Ah[tc // 8]
                ofs = (tc % 8) * 128
                fw.act(k1[:, tc * 64:(tc + 1) * 64], ah[:, ofs:ofs + 64],
                       AF.Exp, accum_out=C.n_k1sum[h][:, tc:tc + 1],
                       tag=f"{seq_tag}k1e{tc}")
            fw.recip(C.n_k1sum[h][:, :], C.n_k1sum[h][:, :])
            for tc in range(16):
                fw.ts(k1[:, tc * 64:(tc + 1) * 64], k1[:, tc * 64:(tc + 1) * 64],
                      C.n_k1sum[h][:, tc:tc + 1], None, ALU.mult)
            for tc in range(16):
                _tc(fw, C, k1T[0:64, tc * 128:(tc + 1) * 128],
                    k1[:, tc * 64:(tc + 1) * 64], C.eye_bf[:, :], tc, 128, 64,
                    tag=f"{seq_tag}k1T{tc}")
            fw.mm(C.BL[0:64, 0:64], qlm, klm, start=True, stop=True)
            fw.act(_sl64(S, K2b)[0:64, :], C.BL[0:64, 0:64], AF.Exp,
                   accum_out=vsm[0:64, 1:2])
            fw.recip(vsm[0:64, 1:2], vsm[0:64, 1:2])
            fw.ts(_sl64(S, K2b)[0:64, :], _sl64(S, K2b)[0:64, :],
                  vsm[0:64, 1:2], None, ALU.mult)
            for n in range(4):
                sl = slice(n * 512, (n + 1) * 512)
                fw.mm(C.Bn[n][0:64, :], qlm, C.kb[hs, sl], start=True, stop=True)
            for hh in range(2):
                fw.act(k3[0:64, hh * 1024:(hh + 1) * 1024], C.Bh[hh][0:64, :], AF.Exp,
                       accum_out=vsm[0:64, 2 + hh:3 + hh], tag=f"{seq_tag}k3e{hh}")
            fw.tt(vsm[0:64, 0:1], vsm[0:64, 2:3], vsm[0:64, 3:4], ALU.add)
            fw.recip(vsm[0:64, 0:1], vsm[0:64, 0:1])
            fw.ts(k3[0:64, :], k3[0:64, :], vsm[0:64, 0:1], None, ALU.mult)
            for tc in range(16):
                _tc(fw, C, k3T[:, tc * 64:(tc + 1) * 64],
                    k3[0:64, tc * 128:(tc + 1) * 128], ident0, tc, 64, 128,
                    tag=f"{seq_tag}k3T{tc}")
            fw.phase = seq_tag + "l1.newton"
            fw.trans(C.PTbB[0:64, 64:128], _sl64(S, K2b)[0:64, :], ident0)
            fw.copy(_sl64(S, K2T)[0:64, :], C.PTbB[0:64, 64:128])
            fw.mm(C.BL[0:1, 0:64], C.allones[0:64, 0:1], _sl64(S, K2b)[0:64, :],
                  start=True, stop=True)
            fw.red(vsm[0:1, 2:3], C.BL[0:1, 0:64], ALU.max)
            fw.recip(vsm[0:1, 2:3], vsm[0:1, 2:3])
            fw.act(C.row_bf[0:1, 0:1], vsm[0:1, 2:3], AF.Copy)
            fw.mm(C.BL[0:64, 64:65], C.ones1[0:1, 0:64], C.row_bf[0:1, 0:1],
                  start=True, stop=True)
            fw.copy(vsm[0:64, 3:4], C.BL[0:64, 64:65])
            fw.ts(_sl64(S, Vb)[0:64, :], _sl64(S, K2T)[0:64, :],
                  vsm[0:64, 3:4], None, ALU.mult)
            for it in range(6):
                fw.mm(C.BL[0:64, 0:64], _sl64(S, K2T)[0:64, :], _sl64(S, Vb)[0:64, :],
                      start=True, stop=True)
                fw.copy(_sl64(S, KVb)[0:64, :], C.BL[0:64, 0:64])
                fw.tt(_sl64(S, Aw)[0:64, :], _sl64(C.n_cI, 0)[0:64, :],
                      C.BL[0:64, 0:64], ALU.subtract)
                fw.trans(C.PTbB[0:64, 256:320], _sl64(S, KVb)[0:64, :], ident0)
                fw.copy(_sl64(S, KVT)[0:64, :], C.PTbB[0:64, 256:320])
                fw.mm(C.BL[0:64, 0:64], _sl64(S, KVT)[0:64, :], _sl64(S, Aw)[0:64, :],
                      start=True, stop=True)
                fw.tt(_sl64(S, Cw)[0:64, :], _sl64(C.n_cI, 1)[0:64, :],
                      C.BL[0:64, 0:64], ALU.subtract)
                fw.mm(C.BL[0:64, 0:64], _sl64(S, KVT)[0:64, :], _sl64(S, Cw)[0:64, :],
                      start=True, stop=True)
                fw.tt(_sl64(S, Ew)[0:64, :], _sl64(C.n_cI, 2)[0:64, :],
                      C.BL[0:64, 0:64], ALU.subtract)
                fw.trans(C.PTbB[0:64, 256:320], _sl64(S, Vb)[0:64, :], ident0)
                fw.copy(_sl64(S, VT)[0:64, :], C.PTbB[0:64, 256:320])
                fw.mm(C.BL[0:64, 0:64], _sl64(S, VT)[0:64, :], _sl64(S, Ew)[0:64, :],
                      start=True, stop=True)
                fw.act(_sl64(S, Vb)[0:64, :], C.BL[0:64, 0:64], AF.Identity,
                       scale=cst(C, 6)[0:64, :], tag=f"{seq_tag}nwt{it}")
            fw.phase = seq_tag + "l1.M"
            for kc in range(16):
                fw.mm(C.BL[0:64, 64:128], k3T[:, kc * 64:(kc + 1) * 64],
                      vtm[:, kc * 65:kc * 65 + 64], start=(kc == 0), stop=(kc == 15))
            fw.copy(_sl64(S, M1b)[0:64, :], C.BL[0:64, 64:128])
            fw.trans(C.PTbB[0:64, 256:320], _sl64(S, Vb)[0:64, :], ident0)
            fw.copy(_sl64(S, VT)[0:64, :], C.PTbB[0:64, 256:320])
            fw.mm(C.BL[0:64, 0:64], _sl64(S, VT)[0:64, :], _sl64(S, M1b)[0:64, :],
                  start=True, stop=True)
            fw.copy(_sl64(S, M2b)[0:64, :], C.BL[0:64, 0:64])
            for n in range(4):
                sl = slice(n * 512, (n + 1) * 512)
                fw.mm(C.Bn[n][0:64, :], _sl64(S, M2b)[0:64, :], k1T[0:64, sl],
                      start=True, stop=True)
            for hh in range(2):
                fw.act(C.svT[pair][hs, hh * 1024:(hh + 1) * 1024], C.Bh[hh][0:64, :],
                       AF.Copy, tag=f"{seq_tag}svl1{pair}.{h}.{hh}")

    for o in range(8):
        fw.phase = seq_tag + "l1.oproj"
        slot = C.wsQ if o % 2 == 0 else C.wsO
        _stream_w_chunk(fw, C, slot, W["w_proj"], o * 128, 128, tag=f"{seq_tag}l1wo{o}")

        def evac(h, ph, o=o):
            hsl = slice(h * 1024, (h + 1) * 1024)
            fw.tt(C.xT[o][:, hsl], C.xT[o][:, hsl], ph[:, :], ALU.add,
                  tag=f"{seq_tag}l1x{o}.{h}")
        _proj_1024(fw, C, slot, evac, C.svT)



def alloc_l1_ffn(nc, stack, fw, C):
    P = C.pool
    C.f_dec = [P[4], P[5], P[6], P[7]]
    C.f_acts = [P[8], P[9], P[10], P[11]]
    C.f_mT = [P[12], P[17], P[0], P[1]]
    C.f_ind = P[13]
    C.f_win = [P[14], P[15]]
    C.f_wout = P[16]
    C.f_binb = C.mean.bitcast(BF16)[0:1, 0:2040]
    C.f_scr = C.var


def l1_ffn(fw, C, W, seq_tag=""):
    layernorm(fw, C, tag=seq_tag + "ffn.")
    fw.dma(C.f_binb, W["b_in"], eng="pool", tag=f"{seq_tag}binb")
    P4 = (C.AL, C.AH, C.BL, C.BH)

    for tg in range(4):
        fw.phase = seq_tag + "ffn"
        tcs = tuple(4 * tg + j for j in range(4))
        for wh in range(2):
            w0 = wh * 1024
            wn = 1024 if wh == 0 else 1016
            for kc in range(8):
                slot = C.f_win[kc % 2]
                fw.dma(slot[:, 0:wn], W["w_inT"][kc * 128:(kc + 1) * 128, w0:w0 + wn],
                       eng="pool", tag=f"{seq_tag}win{tg}.{wh}.{kc}")
                for j, tc in enumerate(tcs):
                    lhsT = C.hT[kc][:, tc * 128:(tc + 1) * 128]
                    for (a, b) in ((0, 512), (512, wn)):
                        if kc == 0:
                            fw.mm(P4[j][:, a:b], C.ones1[0:1, :],
                                  C.f_binb[0:1, w0 + a:w0 + b], start=True, stop=False)
                        fw.mm(P4[j][:, a:b], lhsT, slot[:, a:b],
                              start=False, stop=(kc == 7))
            for j, tc in enumerate(tcs):
                fw.ts(C.f_dec[j][:, w0:w0 + wn], P4[j][:, 0:wn], 0.0, None, ALU.is_gt)
                fw.act(C.f_acts[j][:, w0:w0 + wn], P4[j][:, 0:wn], AF.Silu,
                       tag=f"{seq_tag}fsilu{tc}.{wh}")
        for j, tc in enumerate(tcs):
            ind3 = C.f_ind[:, 0:2040].rearrange("p (r n) -> p r n", n=255)
            dec3 = C.f_dec[j][:, 0:2040].rearrange("p (r n) -> p r n", n=255)
            fw.memset(ind3[:, :, 0:1], 1.0)
            for d in range(7):
                p0 = 2 ** d - 1
                n = 2 ** d
                c0 = 2 ** (d + 1) - 1
                par = ind3[:, :, p0:p0 + n]
                dd = dec3[:, :, p0:p0 + n]
                right = ind3[:, :, c0 + 1:c0 + 2 * n:2]
                left = ind3[:, :, c0:c0 + 2 * n:2]
                fw.tt(right, par, dd, ALU.mult)
                fw.tt(left, par, right, ALU.subtract)
            fw.tt(C.f_acts[j][:, 0:2040], C.f_acts[j][:, 0:2040],
                  C.f_ind[:, 0:2040], ALU.mult, tag=f"{seq_tag}fmask{tc}")
            for kc in range(16):
                wn2 = 2040 - kc * 128 if kc == 15 else 128
                _tc(fw, C, C.f_mT[j][0:wn2, kc * 128:(kc + 1) * 128],
                    C.f_acts[j][:, kc * 128:kc * 128 + wn2], C.eye_bf[:, :],
                    kc + j, 128, wn2, tag=f"{seq_tag}fmT{tc}.{kc}")
        for kc in range(16):
            wn2 = 2040 - kc * 128 if kc == 15 else 128
            half = C.f_wout[:, (kc % 2) * 1024:(kc % 2 + 1) * 1024]
            fw.dma(half[0:wn2, :], W["w_outT"][kc * 128:kc * 128 + wn2, :],
                   eng="pool", tag=f"{seq_tag}wo{tg}.{kc}")
            for j, tc in enumerate(tcs):
                lhsT = C.f_mT[j][0:wn2, kc * 128:(kc + 1) * 128]
                for n in range(2):
                    fw.mm(P4[j][:, n * 512:(n + 1) * 512], lhsT,
                          half[0:wn2, n * 512:(n + 1) * 512],
                          start=(kc == 0), stop=(kc == 15))
        for j, tc in enumerate(tcs):
            fw.act(C.f_scr[:, 0:1024], P4[j][:, :], AF.Copy, tag=f"{seq_tag}fev{tc}")
            for o in range(8):
                bb = P4[j]
                fw.trans(bb[:, 0:128], C.f_scr[:, o * 128:(o + 1) * 128],
                         C.eye_f32[:, :])
                fw.tt(C.xT[o][:, tc * 128:(tc + 1) * 128],
                      C.xT[o][:, tc * 128:(tc + 1) * 128], bb[:, 0:128], ALU.add,
                      tag=f"{seq_tag}fx{tc}.{o}")



NCORES = 8
NSEQ = 4


def _host_consts():
    return {
        "c_allones": np.ones((128, 128), np.float32),
        "c_ones1": np.ones((1, 128), np.float32),
        "c_eye": np.eye(128, dtype=np.float32),
        "c_cst": np.broadcast_to(
            np.array([1e-5, -0.5, 0.125, 2.5, 1e-20, 1.0 / 32, 0.25, 0], np.float32),
            (128, 8)).copy(),
    }


CONST_SHAPES = {"c_allones": [128, 128], "c_ones1": [1, 128],
                "c_eye": [128, 128], "c_cst": [128, 8]}

L0A_SHAPES = {"wqT": [1024, 1024], "wkT": [1024, 1024], "wvT": [1024, 1024],
              "woT": [1024, 1024], "b_l0": [128, 32]}
MOE_SHAPES = {"gwT": [1024, 32], "gb128": [128, 32],
              "wgT": [32, 1024, 512], "wuT": [32, 1024, 512], "wdT": [32, 512, 1024],
              "shg2": [2, 1024, 512], "shu2": [2, 1024, 512], "shd2": [2, 512, 1024]}
L1A_SHAPES = {"w_qkv": [1024, 3072], "w_proj": [1024, 1024]}
L1F_SHAPES = {"w_inT": [1024, 2040], "b_in": [1, 2040], "w_outT": [2040, 1024]}


def _prep_l0a(W_qkv, b_qkv, W_out, b_out):
    qr = np.concatenate([np.arange(h * 192, h * 192 + 64) for h in range(16)])
    out = {
        "wqT": np.ascontiguousarray(W_qkv[qr, :].T),
        "wkT": np.ascontiguousarray(W_qkv[qr + 64, :].T),
        "wvT": np.ascontiguousarray(W_qkv[qr + 128, :].T),
        "woT": np.ascontiguousarray(W_out.T),
    }
    bq = b_qkv[qr].reshape(8, 128).T
    bk = b_qkv[qr + 64].reshape(8, 128).T
    bv = b_qkv[qr + 128].reshape(8, 128).T
    bo = b_out.reshape(8, 128).T
    out["b_l0"] = np.ascontiguousarray(
        np.concatenate([bq, bk, bv, bo], axis=1)).astype(np.float32)
    return out


def _prep_moe(gate_w, gate_bias, w_gate, w_up, w_down, sh_gate, sh_up, sh_down):
    shgT, shuT, shdT = sh_gate.T, sh_up.T, sh_down.T
    return {
        "gwT": np.ascontiguousarray(gate_w.T),
        "gb128": np.broadcast_to(gate_bias[None, :], (128, 32)).astype(np.float32).copy(),
        "wgT": np.ascontiguousarray(w_gate.transpose(0, 2, 1)),
        "wuT": np.ascontiguousarray(w_up.transpose(0, 2, 1)),
        "wdT": np.ascontiguousarray(w_down.transpose(0, 2, 1)),
        "shg2": np.ascontiguousarray(np.stack([shgT[:, 0:512], shgT[:, 512:1024]])),
        "shu2": np.ascontiguousarray(np.stack([shuT[:, 0:512], shuT[:, 512:1024]])),
        "shd2": np.ascontiguousarray(np.stack([shdT[0:512, :], shdT[512:1024, :]])),
    }


def _declare(nc, shapes):
    d = {}
    for n, shp in shapes.items():
        d[n] = nc.dram_tensor(n, list(shp), F32, kind="ExternalInput").ap()
    return d


def _build_block(block):
    nc = bass.Bass("TRN2", target_bir_lowering=False)
    x_d = nc.dram_tensor("xT", [NSEQ, 1024, 2048], F32, kind="ExternalInput").ap()
    y_d = nc.dram_tensor("yT", [NSEQ, 1024, 2048], F32, kind="ExternalOutput").ap()
    dram = _declare(nc, CONST_SHAPES)
    wshapes = {"l0a": L0A_SHAPES, "moe": MOE_SHAPES,
               "l1a": L1A_SHAPES, "l1f": L1F_SHAPES}[block]
    W = _declare(nc, wshapes)

    import contextlib
    with contextlib.ExitStack() as stack:
        fw = FW(nc)
        C = alloc_common(nc, stack, fw)
        if block == "l0a":
            alloc_l0_attn(nc, stack, fw, C)
        elif block == "moe":
            alloc_l0_moe(nc, stack, fw, C)
        elif block == "l1a":
            alloc_l0_attn(nc, stack, fw, C)
            alloc_l1_attn(nc, stack, fw, C)
        else:
            alloc_l1_ffn(nc, stack, fw, C)
        emit_const_init(fw, C, dram)
        if block == "l0a":
            l0_attn_init(fw, C, W)
        elif block == "moe":
            l0_moe_init(fw, C, W)
        elif block == "l1a":
            for i in range(2):
                ap = C.vtm[i].ap().rearrange("p (a n) -> p a n", n=65)
                fw.memset(ap[:, :, 64:65], 1.0)
            l1_attn_init(fw, C)
        for s in range(NSEQ):
            load_xT(fw, C, x_d, s)
            if block == "l0a":
                l0_attn(fw, C, W, seq_tag=f"s{s}.")
            elif block == "moe":
                l0_moe(fw, C, W, seq_tag=f"s{s}.")
            elif block == "l1a":
                l1_attn(fw, C, W, seq_tag=f"s{s}.")
            else:
                l1_ffn(fw, C, W, seq_tag=f"s{s}.")
            store_xT(fw, C, y_d, s)
        fw.finalize(stack)
        fw.emit()
    return nc


def _build_fused():
    nc = bass.Bass("TRN2", target_bir_lowering=False)
    x_d = nc.dram_tensor("xT", [NSEQ, 1024, 2048], F32, kind="ExternalInput").ap()
    y_d = nc.dram_tensor("yT", [NSEQ, 1024, 2048], F32, kind="ExternalOutput").ap()
    dram = _declare(nc, CONST_SHAPES)
    W = {}
    for shapes in (L0A_SHAPES, MOE_SHAPES, L1A_SHAPES, L1F_SHAPES):
        W.update(_declare(nc, shapes))

    import contextlib
    with contextlib.ExitStack() as stack:
        fw = FW(nc)
        C = alloc_common(nc, stack, fw)
        alloc_l0_attn(nc, stack, fw, C)
        alloc_l0_moe(nc, stack, fw, C)
        alloc_l1_attn(nc, stack, fw, C)
        alloc_l1_ffn(nc, stack, fw, C)
        emit_const_init(fw, C, dram)
        l0_attn_init(fw, C, W)
        l0_moe_init(fw, C, W)
        l1_attn_init(fw, C)
        for s in range(NSEQ):
            load_xT(fw, C, x_d, s)
            l0_attn(fw, C, W, seq_tag=f"s{s}a.")
            l0_moe(fw, C, W, seq_tag=f"s{s}m.")
            l1_attn(fw, C, W, seq_tag=f"s{s}n.")
            l1_ffn(fw, C, W, seq_tag=f"s{s}f.")
            store_xT(fw, C, y_d, s)
        fw.finalize(stack)
        fw.emit()
    _flip_ldweights(nc, fw.flip_names)
    return nc


def _flip_ldweights(nc, names):
    import copy as _c
    todo = set(names)
    if not todo:
        return
    for blk in nc.main_func.blocks:
        il = blk.instructions
        hit = False
        for i, ins in enumerate(il):
            if ins.name in todo:
                il[i] = _c.replace(ins, ldweights=False)
                todo.discard(ins.name)
                hit = True
        if hit and not todo:
            break
    assert not todo, f"{len(todo)} skip-ldweights matmuls not found in BIR"


def kernel(L0_attn_x, L0_attn_W_qkv, L0_attn_b_qkv, L0_attn_W_out, L0_attn_b_out,
           L0_ffn_gate_w, L0_ffn_gate_bias, L0_ffn_w_gate, L0_ffn_w_up,
           L0_ffn_w_down, L0_ffn_sh_gate, L0_ffn_sh_up, L0_ffn_sh_down,
           L1_attn_w_qkv, L1_attn_w_proj, L1_ffn_W_in, L1_ffn_b_in, L1_ffn_W_out):
    f32 = lambda a: np.asarray(a, dtype=np.float32)
    x = f32(L0_attn_x)
    xT = np.ascontiguousarray(x.transpose(0, 2, 1))

    consts = _host_consts()
    w_l0a = _prep_l0a(f32(L0_attn_W_qkv), f32(L0_attn_b_qkv),
                      f32(L0_attn_W_out), f32(L0_attn_b_out))
    w_moe = _prep_moe(f32(L0_ffn_gate_w), f32(L0_ffn_gate_bias), f32(L0_ffn_w_gate),
                      f32(L0_ffn_w_up), f32(L0_ffn_w_down), f32(L0_ffn_sh_gate),
                      f32(L0_ffn_sh_up), f32(L0_ffn_sh_down))
    w_l1a = {"w_qkv": f32(L1_attn_w_qkv), "w_proj": f32(L1_attn_w_proj)}
    w_l1f = {"w_inT": np.ascontiguousarray(f32(L1_ffn_W_in).T),
             "b_in": f32(L1_ffn_b_in).reshape(1, 2040),
             "w_outT": np.ascontiguousarray(f32(L1_ffn_W_out).T)}

    shards = [np.ascontiguousarray(xT[i * NSEQ:(i + 1) * NSEQ]) for i in range(NCORES)]
    nc = _build_fused()
    in_maps = []
    for i in range(NCORES):
        m = {"xT": shards[i]}
        m.update(consts)
        m.update(w_l0a)
        m.update(w_moe)
        m.update(w_l1a)
        m.update(w_l1f)
        in_maps.append(m)
    res = run_bass_kernel_spmd(nc, in_maps, core_ids=list(range(NCORES)))
    shards = [np.asarray(r["yT"]) for r in res.results]

    yT = np.concatenate(shards, axis=0)
    return np.ascontiguousarray(yT.transpose(0, 2, 1))



# version 12
# speedup vs baseline: 1.0750x; 1.0750x over previous
import sys
sys.path.insert(0, "/opt/trn_rl_repo")

import numpy as np
import concourse.bass as bass
import concourse.mybir as mybir
from concourse.bass_utils import run_bass_kernel_spmd




F32 = mybir.dt.float32
BF16 = mybir.dt.bfloat16
AF = mybir.ActivationFunctionType
ALU = mybir.AluOpType
AX = mybir.AxisListType

ENGINES = ("pe", "act", "dve", "pool", "sp")
DMAQ = {"sp": "spq", "pool": "poolq", "act": "actq"}


def _ap_key(ap):
    t = getattr(ap, "tensor", None)
    if t is None:
        return ap.name
    return t.name


def _is_psum(ap):
    t = getattr(ap, "tensor", ap)
    sp = getattr(t, "space", None)
    return sp is not None and str(getattr(sp, "value", sp)) == "psum"


class _Op:
    __slots__ = ("i", "eng", "fn", "reads", "writes", "dma", "tag",
                 "deps", "token", "waits", "vc", "psum_reads", "pre_wait",
                 "phase", "bir", "skip_ldw", "lw_sig", "lw_key")

    def __init__(self, i, eng, fn, reads, writes, dma, tag):
        self.i = i
        self.eng = eng
        self.fn = fn
        self.reads = reads
        self.writes = writes
        self.dma = dma
        self.tag = tag
        self.deps = []
        self.token = None
        self.waits = []
        self.vc = None
        self.skip_ldw = False
        self.lw_sig = None
        self.lw_key = None


def _lw_sig(ap):
    try:
        return (ap.name, ap.offset, tuple(ap.shape), str(ap.stride()),
                ap.base_partition(), str(ap.dtype))
    except Exception:
        return None


class FW:
    def __init__(self, nc):
        self.nc = nc
        self.phase = ""
        self.ops = []
        self._lw = {}
        self._rd = {}
        self.sems = {}
        self.max_tok = {}

    def _rec(self, eng, fn, reads, writes, dma=False, tag=""):
        rl = [a for a in reads if a is not None and not isinstance(a, (int, float))]
        rk = [_ap_key(a) for a in rl]
        wk = [_ap_key(a) for a in writes]
        op = _Op(len(self.ops), eng, fn, rk, wk, dma, tag)
        op.phase = self.phase
        op.psum_reads = [_ap_key(a) for a in rl if _is_psum(a)]
        self.ops.append(op)
        return op

    def mm(self, out, lhsT, rhs, start=True, stop=True, tag=""):
        op = self._rec("pe", lambda e: e.matmul(out, lhsT, rhs, start=start, stop=stop),
                       [lhsT, rhs], [out], tag=tag)
        op.lw_sig = _lw_sig(lhsT)
        op.lw_key = _ap_key(lhsT)

    def trans(self, out, in_, ident, tag=""):
        self._rec("pe", lambda e: e.transpose(out, in_, ident), [in_, ident], [out], tag=tag)

    def act(self, out, in_, func, bias=0.0, scale=1.0, accum_out=None, tag=""):
        def f(e):
            kw = {}
            if accum_out is not None:
                kw["accum_out"] = accum_out
            return e.activation(out, in_, func, bias=bias, scale=scale, **kw)
        wr = [out] if accum_out is None else [out, accum_out]
        self._rec("act", f, [in_, bias if not isinstance(bias, float) else None,
                             scale if not isinstance(scale, float) else None], wr, tag=tag)

    def eng_op(self, eng, fn, reads, writes, tag=""):
        self._rec(eng, fn, reads, writes, tag=tag)

    def tt(self, out, a, b, op, eng="dve", tag=""):
        self._rec(eng, lambda e: e.tensor_tensor(out, a, b, op), [a, b], [out], tag=tag)

    def ts(self, out, in_, s1, s2, op0, op1=ALU.bypass, eng="dve", tag=""):
        self._rec(eng, lambda e: e.tensor_scalar(out, in_, s1, s2, op0, op1),
                  [in_, s1 if not isinstance(s1, (int, float)) else None,
                   s2 if not isinstance(s2, (int, float)) else None], [out], tag=tag)

    def tss(self, out, in_, s, op, eng="dve", tag=""):
        self._rec(eng, lambda e: e.tensor_single_scalar(out, in_, s, op),
                  [in_, s if not isinstance(s, (int, float)) else None], [out], tag=tag)

    def stt(self, out, in0, s, in1, op0, op1, eng="dve", tag=""):
        self._rec(eng, lambda e: e.scalar_tensor_tensor(out, in0, s, in1, op0, op1),
                  [in0, in1, s if not isinstance(s, (int, float)) else None], [out], tag=tag)

    def red(self, out, in_, op, negate=False, eng="dve", tag=""):
        self._rec(eng, lambda e: e.tensor_reduce(out, in_, AX.X, op, negate=negate),
                  [in_], [out], tag=tag)

    def copy(self, out, in_, eng="dve", tag=""):
        self._rec(eng, lambda e: e.tensor_copy(out, in_), [in_], [out], tag=tag)

    def recip(self, out, in_, tag=""):
        self._rec("dve", lambda e: e.reciprocal(out, in_), [in_], [out], tag=tag)

    def memset(self, ap, v, eng="dve", tag=""):
        self._rec(eng, lambda e: e.memset(ap, v), [], [ap], tag=tag)

    def dma(self, out, in_, eng="sp", tag=""):
        self._rec(eng, lambda e: e.dma_start(out, in_), [in_], [out], dma=True, tag=tag)

    def barrier_note(self):
        pass

    def finalize(self, stack):
        ops = self.ops
        lw = {}
        rd = {}
        psum_rd = {}
        pe_sig = pe_ver = None
        for op in ops:
            dep = set()
            for k in op.reads:
                if k in lw:
                    dep.add(lw[k])
            for k in op.psum_reads:
                for (r, e) in psum_rd.get(k, ()):
                    if e != op.eng:
                        dep.add(r)
            for k in op.writes:
                if k in lw:
                    dep.add(lw[k])
                for r in rd.get(k, ()):
                    dep.add(r)
            dep.discard(op.i)
            last_per_eng = {}
            dma_deps = set()
            for d in dep:
                dop = ops[d]
                if dop.dma:
                    dma_deps.add(d)
                    continue
                if dop.eng == "pe" and op.eng == "pe" and not op.dma:
                    continue
                if d > last_per_eng.get(dop.eng, -1):
                    last_per_eng[dop.eng] = d
            op.deps = sorted(dma_deps | set(last_per_eng.values()))
            if op.eng == "pe":
                if op.lw_sig is not None:
                    ver = lw.get(op.lw_key, -1)
                    if pe_sig is not None and op.lw_sig == pe_sig and ver == pe_ver:
                        op.skip_ldw = True
                    pe_sig, pe_ver = op.lw_sig, ver
                else:
                    pe_sig = pe_ver = None
            for k in op.reads:
                rd.setdefault(k, []).append(op.i)
            for k in op.psum_reads:
                psum_rd.setdefault(k, []).append((op.i, op.eng))
            for k in op.writes:
                lw[k] = op.i
                rd[k] = []
                psum_rd[k] = []

        needed = set()
        for op in ops:
            for d in op.deps:
                needed.add(d)
        NSEM = 16
        counters = {}
        qcount = {}
        for op in ops:
            if op.dma:
                q = DMAQ[op.eng]
                i = qcount.get(q, 0)
                qcount[q] = i + 1
                stream = f"{q}{i % NSEM}"
                epoch = i // NSEM
                op.token = (stream, 16 * (epoch + 1))
                counters[stream] = 16 * (epoch + 1)
                op.pre_wait = (stream, 16 * epoch) if epoch > 0 else None
            else:
                op.pre_wait = None
                if op.i in needed:
                    stream = op.eng
                    counters[stream] = counters.get(stream, 0) + 1
                    op.token = (stream, counters[stream])
        self.max_tok = dict(counters)

        evc = {e: {} for e in ENGINES}
        svc = {e: {} for e in ENGINES}
        for op in ops:
            know = evc[op.eng]
            waits = []
            mvc = dict(svc[op.eng])
            for d in op.deps:
                dop = ops[d]
                if dop.token is None:
                    continue
                s, v = dop.token
                if know.get(s, 0) < v:
                    waits.append((s, v))
                    know[s] = v
                    for ks, kv in dop.vc.items():
                        if know.get(ks, 0) < kv:
                            know[ks] = kv
                for ks, kv in dop.vc.items():
                    if mvc.get(ks, 0) < kv:
                        mvc[ks] = kv
            if op.pre_wait is not None:
                s, v = op.pre_wait
                if know.get(s, 0) < v:
                    waits.append((s, v))
                    know[s] = v
            wmax = {}
            for s, v in waits:
                wmax[s] = max(wmax.get(s, 0), v)
            op.waits = sorted(wmax.items())
            svc_update = dict(mvc)
            if op.token is not None:
                s, v = op.token
                mvc = dict(mvc)
                mvc[s] = v
                if not op.dma:
                    svc_update[s] = v
            op.vc = mvc
            svc[op.eng] = svc_update

        streams = set()
        for op in ops:
            if op.token:
                streams.add(op.token[0])
        for s in sorted(streams):
            self.sems[s] = stack.enter_context(self.nc.semaphore(f"s_{s}"))

        return self

    def emit(self, final_waits=True):
        nc = self.nc
        sems = self.sems
        by_eng = {e: [] for e in ENGINES}
        for op in self.ops:
            by_eng[op.eng].append(op)
        self.flip_names = []

        def body(eng_name):
            def run(e):
                for op in by_eng[eng_name]:
                    for s, v in op.waits:
                        e.wait_ge(sems[s], v)
                    inst = op.fn(e)
                    op.bir = inst.ins.name
                    if op.skip_ldw:
                        self.flip_names.append(op.bir)
                    if op.token is not None:
                        s, v = op.token
                        inst.then_inc(sems[s], 16 if op.dma else 1)
                if eng_name == "sp" and final_waits:
                    for s, v in self.max_tok.items():
                        if s in sems:
                            e.wait_ge(sems[s], v)
            return run

        with nc.Block() as block:
            block.tensor(body("pe"))
            block.scalar(body("act"))
            block.vector(body("dve"))
            block.gpsimd(body("pool"))
            block.sync(body("sp"))

    def stats(self):
        n = {e: 0 for e in ENGINES}
        w = 0
        t = 0
        for op in self.ops:
            n[op.eng] += 1
            w += len(op.waits)
            t += 1 if op.token else 0
        return {"ops": len(self.ops), "per_engine": n, "waits": w, "tokens": t}





class Ctx:
    pass


NPOOL = 18


def alloc_common(nc, stack, fw):
    C = Ctx()
    sb = lambda name, shape, dt: stack.enter_context(nc.sbuf_tensor(name, shape, dt))
    ps = lambda name, shape, dt: stack.enter_context(nc.psum_tensor(name, shape, dt))

    C.xT = [sb(f"xT{c}", [128, 2048], F32) for c in range(8)]
    C.hT = [sb(f"hT{c}", [128, 2048], BF16) for c in range(8)]
    C.pool = [sb(f"pl{i}", [128, 2048], BF16) for i in range(NPOOL)]

    C.AL = ps("psAL", [128, 1024], F32)
    C.AH = ps("psAH", [128, 1024], F32)
    C.BL = ps("psBL", [128, 1024], F32)
    C.BH = ps("psBH", [128, 1024], F32)
    C.An = [C.AL[:, 0:512], C.AL[:, 512:1024], C.AH[:, 0:512], C.AH[:, 512:1024]]
    C.Bn = [C.BL[:, 0:512], C.BL[:, 512:1024], C.BH[:, 0:512], C.BH[:, 512:1024]]
    C.Ah = [C.AL, C.AH]
    C.Bh = [C.BL, C.BH]
    C.PTbB = C.BL.bitcast(BF16)
    C.PTb4 = [C.AL.bitcast(BF16), C.AH.bitcast(BF16),
              C.BL.bitcast(BF16), C.BH.bitcast(BF16)]
    C.grp = 0

    C.mean = sb("ln_mean", [128, 2048], F32)
    C.var = sb("ln_var", [128, 2048], F32)
    C.xb = [sb(f"ln_xb{i}", [128, 2048], BF16) for i in range(2)]

    C.scores_sb = C.mean[0:32, :]
    C.combT = C.var[0:32, :]
    C.row_f32 = C.var[64:65, :]
    C.row_bf = C.mean.bitcast(BF16)[0:1, 0:2048]

    C.allones = sb("allones", [128, 128], BF16)
    C.ones1 = sb("ones1", [1, 128], BF16)
    C.eye_bf = sb("eye_bf", [128, 128], BF16)
    C.eye_f32 = sb("eye_f32", [128, 128], F32)
    C.cst = sb("cst", [128, 8], F32)
    C.EPS, C.NEGHALF, C.QSCALE, C.SCALE25, C.EPS20, C.INV32 = range(6)
    return C


def cst(C, i):
    return C.cst[:, i:i + 1]


def emit_const_init(fw, C, dram):
    fw.dma(C.allones[:, :], dram["c_allones"], eng="pool")
    fw.dma(C.ones1[:, :], dram["c_ones1"], eng="pool")
    fw.dma(C.eye_bf[:, :], dram["c_eye"], eng="pool")
    fw.dma(C.eye_f32[:, :], dram["c_eye"], eng="sp")
    fw.dma(C.cst[:, :], dram["c_cst"], eng="sp")


def load_xT(fw, C, x_dram, s):
    src = x_dram.rearrange("b (c p) t -> b c p t", p=128)
    for c in range(8):
        fw.dma(C.xT[c][:, :], src[s, c, :, :], eng="sp", tag=f"ldx{c}")


def store_xT(fw, C, y_dram, s):
    dst = y_dram.rearrange("b (c p) t -> b c p t", p=128)
    for c in range(8):
        fw.dma(dst[s, c, :, :], C.xT[c][:, :], eng="sp", tag=f"stx{c}")


def _halves(full):
    return (full[:, 0:1024], full[:, 1024:2048])


def layernorm(fw, C, tag=""):
    fw.phase = tag + "LN"
    """hT = LayerNorm(xT). Stats via all-ones matmul (sums broadcast on all
    partitions). h computed as x*rstd - (mean*rstd)."""
    for c in range(8):
        xb = C.xb[c % 2]
        fw.act(xb[:, :], C.xT[c][:, :], AF.Copy, tag=f"{tag}lnc{c}")
        for n in range(4):
            sl = slice(n * 512, (n + 1) * 512)
            fw.mm(C.An[n], C.allones[:, :], xb[:, sl], start=(c == 0), stop=(c == 7))
    for h in range(2):
        fw.ts(C.mean[:, h * 1024:(h + 1) * 1024], C.Ah[h][:, :], 1.0 / 1024.0, None,
              ALU.mult, tag=f"{tag}lnmean{h}")
    for c in range(8):
        xq = C.xb[c % 2]
        fw.act(xq[:, :], C.xT[c][:, :], AF.Square, tag=f"{tag}lnsq{c}")
        for n in range(4):
            sl = slice(n * 512, (n + 1) * 512)
            fw.mm(C.Bn[n], C.allones[:, :], xq[:, sl], start=(c == 0), stop=(c == 7))
    for h in range(2):
        hsl = slice(h * 1024, (h + 1) * 1024)
        fw.act(C.var[:, hsl], C.mean[:, hsl], AF.Square, tag=f"{tag}lnmsq{h}")
        fw.stt(C.var[:, hsl], C.Bh[h][:, :], 1.0 / 1024.0, C.var[:, hsl],
               ALU.mult, ALU.subtract)
        fw.act(C.var[:, hsl], C.var[:, hsl], AF.Ln, bias=cst(C, C.EPS))
        fw.act(C.var[:, hsl], C.var[:, hsl], AF.Exp, scale=cst(C, C.NEGHALF))
        fw.tt(C.mean[:, hsl], C.mean[:, hsl], C.var[:, hsl], ALU.mult)
    for c in range(8):
        eng = "dve" if c % 2 == 0 else "pool"
        fw.tt(C.hT[c][:, :], C.xT[c][:, :], C.var[:, :], ALU.mult, eng=eng)
        fw.tt(C.hT[c][:, :], C.hT[c][:, :], C.mean[:, :], ALU.subtract, eng=eng,
              tag=f"{tag}lnh{c}")


def _tc(fw, C, dst, in_, ident, k, rows, cols, rot=(0, 1, 2, 3), tag=""):
    reg = C.PTb4[rot[k % len(rot)]][0:cols, 0:rows]
    fw.trans(reg, in_, ident)
    if k % 2 == 0:
        fw.copy(dst, reg, tag=tag)
    else:
        fw.act(dst, reg, AF.Copy, tag=tag)


def _stream_w_chunk(fw, C, dst_ap, w_dram, col0, ncol, tag=""):
    src = w_dram.rearrange("(a p) n -> p a n", p=128)
    fw.dma(dst_ap.rearrange("p (a n) -> p a n", n=ncol),
           src[:, :, col0:col0 + ncol], eng="pool", tag=tag)


def _proj_1024(fw, C, slot, dest_half_ops, rhs_chunks):
    Pn, Ph = (C.An, C.Ah) if C.grp == 0 else (C.Bn, C.Bh)
    C.grp ^= 1
    for kc in range(8):
        lhsT = slot[:, kc * 128:(kc + 1) * 128]
        for n in range(4):
            sl = slice(n * 512, (n + 1) * 512)
            fw.mm(Pn[n], lhsT, rhs_chunks[kc][:, sl], start=(kc == 0), stop=(kc == 7))
    for h in range(2):
        dest_half_ops(h, Ph[h])



def alloc_l0_attn(nc, stack, fw, C):
    sb = lambda name, shape, dt: stack.enter_context(nc.sbuf_tensor(name, shape, dt))
    P = C.pool
    C.qb, C.kb, C.vb = P[0], P[1], P[2]
    C.pTx = [P[3], P[12]]
    C.svT = P[4:12]
    C.wsQ = P[13][:, 0:1024]
    C.wsK = P[13][:, 1024:2048]
    C.wsV = P[14][:, 0:1024]
    C.wsO = P[14][:, 1024:2048]
    C.vtm = [sb("vtm0", [128, 16 * 65], BF16),
             C.xb[0][:, 0:16 * 65]]
    C.bias2 = sb("bias2", [128, 32], F32)


def l0_attn_init(fw, C, W):
    fw.dma(C.bias2[:, :], W["b_l0"], eng="sp")
    fw.ts(C.bias2[:, 0:8], C.bias2[:, 0:8], 0.125, None, ALU.mult)


def _vtm_ones(fw, C, h):
    ap = C.vtm[h].rearrange("p (a n) -> p a n", n=65)
    fw.memset(ap[:, :, 64:65], 1.0)


def l0_attn(fw, C, W, seq_tag=""):
    layernorm(fw, C, tag=seq_tag + "L0a.")

    for pair in range(8):
        fw.phase = seq_tag + "l0.qkv"
        for wname, slot in (("wqT", C.wsQ), ("wkT", C.wsK), ("wvT", C.wsV)):
            _stream_w_chunk(fw, C, slot, W[wname], pair * 128, 128,
                            tag=f"{seq_tag}w{wname}{pair}")
        for (slot, dst, bofs, scale_ap) in (
            (C.wsQ, C.qb, 0, cst(C, C.QSCALE)),
            (C.wsK, C.kb, 8, None),
            (C.wsV, C.vb, 16, None),
        ):
            def evac(h, ph, dst=dst, bofs=bofs, scale_ap=scale_ap):
                kw = {"bias": C.bias2[:, bofs + pair:bofs + pair + 1]}
                if scale_ap is not None:
                    kw["scale"] = scale_ap
                fw.act(_halves(dst)[h], ph[:, :], AF.Identity,
                       tag=f"{seq_tag}ev{bofs}.{pair}", **kw)
            _proj_1024(fw, C, slot, evac, C.hT)

        for h in (0, 1):
            fw.phase = seq_tag + "l0.head"
            hs = slice(64 * h, 64 * h + 64)
            vtm = C.vtm[h]
            ident = C.eye_bf[hs, hs]
            _vtm_ones(fw, C, h)
            for kc in range(16):
                _tc(fw, C, vtm[:, kc * 65:kc * 65 + 64],
                    C.vb[hs, kc * 128:(kc + 1) * 128], ident, kc, 64, 128,
                    tag=f"{seq_tag}vtm")
            fw.phase = seq_tag + "l0.sc"
            sv = C.svT[pair]
            for qh in range(2):
                pvh = C.Bh[qh]
                pvn = (C.Bn[2 * qh], C.Bn[2 * qh + 1])
                for kc in range(17):
                    if kc < 16:
                        lhsT = C.kb[hs, kc * 128:(kc + 1) * 128]
                        sb_ = C.Ah[kc % 2]
                        pt = C.pTx[kc % 2]
                        for n in range(2):
                            fw.mm(sb_[:, n * 512:(n + 1) * 512], lhsT,
                                  C.qb[hs, qh * 1024 + n * 512:
                                       qh * 1024 + (n + 1) * 512],
                                  start=True, stop=True)
                        fw.act(pt[:, 0:1024], sb_[:, :], AF.Exp,
                               tag=f"{seq_tag}exp{pair}.{h}.{qh}.{kc}")
                    if kc >= 1:
                        j = kc - 1
                        ptj = C.pTx[j % 2]
                        for n in range(2):
                            fw.mm(pvn[n][0:65, :], vtm[:, j * 65:(j + 1) * 65],
                                  ptj[:, n * 512:(n + 1) * 512],
                                  start=(j == 0), stop=(j == 15))
                hsl = slice(qh * 1024, (qh + 1) * 1024)
                fw.recip(C.row_f32[0:1, hsl], pvh[64:65, :])
                fw.act(C.row_bf[0:1, hsl], C.row_f32[0:1, hsl], AF.Copy)
                fw.act(sv[hs, hsl], pvh[0:64, :], AF.Copy,
                       tag=f"{seq_tag}svev{pair}.{h}.{qh}")
                for n in range(2):
                    fw.mm(pvn[n], C.ones1[0:1, :],
                          C.row_bf[0:1, qh * 1024 + n * 512:qh * 1024 + (n + 1) * 512],
                          start=True, stop=True)
                fw.tt(sv[hs, hsl], sv[hs, hsl], pvh[0:64, :], ALU.mult,
                      tag=f"{seq_tag}svn{pair}.{h}.{qh}")

    for o in range(8):
        fw.phase = seq_tag + "l0.oproj"
        slot = C.wsQ if o % 2 == 0 else C.wsO
        _stream_w_chunk(fw, C, slot, W["woT"], o * 128, 128, tag=f"{seq_tag}wo{o}")

        def evac(h, ph, o=o):
            hsl = slice(h * 1024, (h + 1) * 1024)
            fw.stt(C.xT[o][:, hsl], ph[:, :], C.bias2[:, 24 + o:24 + o + 1],
                   C.xT[o][:, hsl], ALU.add, ALU.add, tag=f"{seq_tag}xadd{o}.{h}")
        _proj_1024(fw, C, slot, evac, C.svT)



def alloc_l0_moe(nc, stack, fw, C):
    sb = lambda name, shape, dt: stack.enter_context(nc.sbuf_tensor(name, shape, dt))
    P = C.pool
    C.guc = [[P[e * 4 + ic] for ic in range(4)] for e in range(2)]
    C.c_sb = [P[8], P[9]]
    C.sg = P[10]
    C.wg_ic = [P[11], P[12]]
    C.wu_ic = [P[13], P[14]]
    C.wd_sl = P[15]
    C.gwT = sb("gw_sb", [128, 256], BF16)
    C.gb_sb = sb("gb_sb", [128, 32], F32)
    C.scores_tm = sb("scores_tm", [128, 512], F32)
    C.sc_tm = sb("sc_tm", [128, 512], F32)
    C.comb_tm = sb("comb_tm", [128, 512], F32)
    P0f = P[0].bitcast(F32)
    P1f = P[1].bitcast(F32)
    P2f = P[2].bitcast(F32)
    C.r_msk = P0f[:, 0:512]
    C.r_m8 = P0f[:, 512:1024]
    C.r_eq = P1f[:, 0:512]
    C.r_num = P1f[:, 512:1024]
    C.r_gs = P2f[:, 0:128]
    C.r_t1 = P2f[:, 128:256]
    C.r_gm = P2f[:, 256:384]
    C.r_mx = P2f[:, 384:400]
    C.r_den = P2f[:, 400:416]
    C.r_h = (P0f, P1f, P2f)


def l0_moe_init(fw, C, W):
    src = W["gwT"].rearrange("(a p) n -> p a n", p=128)
    fw.dma(C.gwT.ap().rearrange("p (a n) -> p a n", n=32), src, eng="pool")
    fw.dma(C.gb_sb[:, :], W["gb128"], eng="sp")


def _routing(fw, C, seq_tag=""):
    fw.phase = seq_tag + "routing"
    """DeepSeek router: sigmoid scores, group top-2-sum, top-4 groups,
    top-8 experts, normalized+scaled combine. All token chunks batched:
    [128 tokens-per-chunk, 16 chunks, ...] on the free dim."""
    import concourse.bass as _bass
    for kc in range(8):
        lhsT = C.gwT[:, kc * 32:(kc + 1) * 32]
        for n in range(4):
            sl = slice(n * 512, (n + 1) * 512)
            fw.mm(C.An[n][0:32, :], lhsT, C.hT[kc][:, sl], start=(kc == 0),
                  stop=(kc == 7))
    for h in range(2):
        fw.act(C.scores_sb[0:32, h * 1024:(h + 1) * 1024], C.Ah[h][0:32, :],
               AF.Sigmoid, tag=f"{seq_tag}sig{h}")
    for tc in range(16):
        ts32 = slice(tc * 32, (tc + 1) * 32)
        bb = C.Bh[tc % 2]
        fw.trans(bb[:, 0:32], C.scores_sb[0:32, tc * 128:(tc + 1) * 128],
                 C.eye_f32[0:32, 0:32])
        if tc % 2 == 0:
            fw.copy(C.scores_tm[:, ts32], bb[:, 0:32], tag=f"{seq_tag}sctm{tc}")
        else:
            fw.act(C.scores_tm[:, ts32], bb[:, 0:32], AF.Copy,
                   tag=f"{seq_tag}sctm{tc}")
    def v3(t, e):
        return t.rearrange("p (t e) -> p t e", e=e)
    gbb = _bass.AP(C.gb_sb, 0, [[32, 128], [0, 16], [1, 32]])
    fw.eng_op("dve", lambda eng: eng.tensor_tensor(
        v3(C.scores_tm[:, :], 32), v3(C.scores_tm[:, :], 32), gbb, ALU.add),
        [C.scores_tm[:, :], C.gb_sb[:, :]], [C.sc_tm[:, :]], tag=f"{seq_tag}scb")
    fw.ops.pop()
    fw.eng_op("dve", lambda eng: eng.tensor_tensor(
        v3(C.sc_tm[:, :], 32), v3(C.scores_tm[:, :], 32), gbb, ALU.add),
        [C.scores_tm[:, :], C.gb_sb[:, :]], [C.sc_tm[:, :]], tag=f"{seq_tag}scb")
    sc4 = C.sc_tm[:, :].rearrange("p (t g e) -> p t g e", g=8, e=4)
    gs3 = v3(C.r_gs, 8)
    t13 = v3(C.r_t1, 8)
    first = True
    for i in range(4):
        for j in range(i + 1, 4):
            fw.tt(t13, sc4[:, :, :, i], sc4[:, :, :, j], ALU.add)
            if first:
                fw.copy(C.r_gs, C.r_t1)
                first = False
            else:
                fw.tt(gs3, gs3, t13, ALU.max)
    fw.memset(C.r_gm, 0.0)
    mxb8 = _bass.AP(C.r_mx.tensor, C.r_mx.offset, [[1024, 128], [1, 16], [0, 8]])
    gm3 = v3(C.r_gm, 8)
    eq3 = v3(C.r_eq[:, 0:128], 8)
    for it in range(4):
        fw.red(C.r_mx, gs3, ALU.max)
        fw.tt(eq3, gs3, mxb8, ALU.is_equal)
        fw.tt(gm3, gm3, eq3, ALU.add)
        fw.stt(C.r_gs, C.r_eq[:, 0:128], -2e9, C.r_gs, ALU.mult, ALU.add)
    msk4 = C.r_msk.rearrange("p (t g e) -> p t g e", g=8, e=4)
    gmb4 = _bass.AP(C.r_gm.tensor, C.r_gm.offset, [[1024, 128], [8, 16], [1, 8], [0, 4]])
    fw.eng_op("dve", lambda eng: eng.tensor_tensor(msk4, sc4, gmb4, ALU.mult),
              [C.sc_tm[:, :], C.r_gm], [C.r_msk], tag=f"{seq_tag}msk")
    fw.memset(C.r_m8, 0.0)
    msk3 = v3(C.r_msk, 32)
    m83 = v3(C.r_m8, 32)
    eq32 = v3(C.r_eq, 32)
    mxb32 = _bass.AP(C.r_mx.tensor, C.r_mx.offset, [[1024, 128], [1, 16], [0, 32]])
    for it in range(8):
        fw.red(C.r_mx, msk3, ALU.max)
        fw.tt(eq32, msk3, mxb32, ALU.is_equal)
        fw.tt(m83, m83, eq32, ALU.add)
        fw.stt(C.r_msk, C.r_eq, -2e9, C.r_msk, ALU.mult, ALU.add)
    fw.tt(C.r_num, C.r_m8, C.scores_tm[:, :], ALU.mult)
    fw.red(C.r_den, v3(C.r_num, 32), ALU.add)
    fw.ts(C.r_den, C.r_den, 1e-20, None, ALU.add)
    fw.recip(C.r_den, C.r_den)
    fw.ts(C.r_den, C.r_den, 2.5, None, ALU.mult)
    denb = _bass.AP(C.r_den.tensor, C.r_den.offset, [[1024, 128], [1, 16], [0, 32]])
    fw.eng_op("dve", lambda eng: eng.tensor_tensor(
        v3(C.comb_tm[:, :], 32), v3(C.r_num, 32), denb, ALU.mult),
        [C.r_num, C.r_den], [C.comb_tm[:, :]], tag=f"{seq_tag}comb")
    for tc in range(16):
        ts32 = slice(tc * 32, (tc + 1) * 32)
        bb = C.Bh[tc % 2]
        fw.trans(bb[0:32, 0:128], C.comb_tm[:, ts32], C.eye_f32[:, :])
        if tc % 2 == 0:
            fw.copy(C.combT[0:32, tc * 128:(tc + 1) * 128], bb[0:32, 0:128],
                    tag=f"{seq_tag}cT{tc}")
        else:
            fw.act(C.combT[0:32, tc * 128:(tc + 1) * 128], bb[0:32, 0:128],
                   AF.Copy, tag=f"{seq_tag}cT{tc}")


def l0_moe(fw, C, W, seq_tag=""):
    layernorm(fw, C, tag=seq_tag + "moe.")
    _routing(fw, C, seq_tag=seq_tag)

    def dma_gu(which, par, src, ic, tag):
        slot = (C.wg_ic if which == "g" else C.wu_ic)[par]
        dst = slot[:, (ic % 2) * 1024:(ic % 2 + 1) * 1024]
        sr = src.rearrange("(a p) n -> p a n", p=128)
        fw.dma(dst.rearrange("p (a n) -> p a n", n=128),
               sr[:, :, ic * 128:(ic + 1) * 128], eng="pool", tag=tag)

    for pair in range(17):
        shared = (pair == 16)
        es = (2 * pair, 2 * pair + 1) if not shared else (0, 1)
        srcs = []
        for par, e in enumerate(es):
            if shared:
                srcs.append((W["shg2"][e], W["shu2"][e], W["shd2"][e]))
            else:
                srcs.append((W["wgT"][e], W["wuT"][e], W["wdT"][e]))
        for par, e in enumerate(es):
            for ic in (0, 1):
                dma_gu("g", par, srcs[par][0], ic, f"{seq_tag}wg{e}.{ic}")
                dma_gu("u", par, srcs[par][1], ic, f"{seq_tag}wu{e}.{ic}")
        def emit_csb():
            if shared:
                return
            for par2, e2 in enumerate(es):
                fw.dma(C.row_f32, C.combT[e2:e2 + 1, :], eng="act",
                       tag=f"{seq_tag}crow{e2}")
                fw.act(C.row_bf, C.row_f32, AF.Copy)
                for n in range(4):
                    sl = slice(n * 512, (n + 1) * 512)
                    fw.mm(C.Bn[n], C.ones1[0:1, :], C.row_bf[0:1, sl],
                          start=True, stop=True)
                for h in range(2):
                    fw.act(_halves(C.c_sb[par2])[h], C.Bh[h][:, :], AF.Copy,
                           tag=f"{seq_tag}csb{e2}.{h}")
        for par, e in enumerate(es):
            fw.phase = seq_tag + "moe.gu"
            for ic in range(4):
                gslot = C.wg_ic[par][:, (ic % 2) * 1024:(ic % 2 + 1) * 1024]
                uslot = C.wu_ic[par][:, (ic % 2) * 1024:(ic % 2 + 1) * 1024]
                for kc in range(8):
                    lhsT = gslot[:, kc * 128:(kc + 1) * 128]
                    for n in range(4):
                        sl = slice(n * 512, (n + 1) * 512)
                        fw.mm(C.An[n], lhsT, C.hT[kc][:, sl], start=(kc == 0),
                              stop=(kc == 7))
                if ic < 2:
                    dma_gu("g", par, srcs[par][0], ic + 2, f"{seq_tag}wg{e}.{ic+2}")
                if par == 0 and ic == 0:
                    emit_csb()
                for h in range(2):
                    fw.act(_halves(C.sg)[h], C.Ah[h][:, :], AF.Silu,
                           tag=f"{seq_tag}silu{e}.{ic}.{h}")
                for kc in range(8):
                    lhsT = uslot[:, kc * 128:(kc + 1) * 128]
                    for n in range(4):
                        sl = slice(n * 512, (n + 1) * 512)
                        fw.mm(C.Bn[n], lhsT, C.hT[kc][:, sl], start=(kc == 0),
                              stop=(kc == 7))
                if ic < 2:
                    dma_gu("u", par, srcs[par][1], ic + 2, f"{seq_tag}wu{e}.{ic+2}")
                for h in range(2):
                    gh = _halves(C.guc[par][ic])[h]
                    guc_eng = "dve"
                    fw.act(gh, C.Bh[h][:, :], AF.Copy, tag=f"{seq_tag}uev{e}.{ic}.{h}")
                    fw.tt(gh, gh, _halves(C.sg)[h], ALU.mult, eng=guc_eng)
                    if not shared:
                        fw.tt(gh, gh, _halves(C.c_sb[par])[h], ALU.mult, eng=guc_eng,
                              tag=f"{seq_tag}guc{e}.{ic}.{h}")
        fw.phase = seq_tag + "moe.down"

        def dma_wd(o2):
            half_ = C.wd_sl[:, (o2 % 2) * 1024:(o2 % 2 + 1) * 1024]
            for par2, e2 in enumerate(es):
                dsr = srcs[par2][2][:, o2 * 128:(o2 + 1) * 128].rearrange(
                    "(a p) n -> p a n", p=128)
                fw.dma(half_[:, par2 * 512:(par2 + 1) * 512].rearrange(
                    "p (a n) -> p a n", n=128), dsr,
                    eng="pool", tag=f"{seq_tag}wd{e2}.{o2}")
        dma_wd(0)
        dma_wd(1)
        for o in range(8):
            half = C.wd_sl[:, (o % 2) * 1024:(o % 2 + 1) * 1024]
            Pn, Ph = (C.An, C.Ah) if o % 2 == 0 else (C.Bn, C.Bh)
            for par in range(2):
                for ic in range(4):
                    lhsT = half[:, par * 512 + ic * 128:par * 512 + (ic + 1) * 128]
                    for n in range(4):
                        sl = slice(n * 512, (n + 1) * 512)
                        fw.mm(Pn[n], lhsT, C.guc[par][ic][:, sl],
                              start=(par == 0 and ic == 0), stop=(par == 1 and ic == 3))
            if o < 6:
                dma_wd(o + 2)
            for h in range(2):
                hsl = slice(h * 1024, (h + 1) * 1024)
                fw.tt(C.xT[o][:, hsl], C.xT[o][:, hsl], Ph[h][:, :], ALU.add,
                      tag=f"{seq_tag}moex{o}.{h}")



def alloc_l1_attn(nc, stack, fw, C):
    sb = lambda name, shape, dt: stack.enter_context(nc.sbuf_tensor(name, shape, dt))
    P = C.pool
    C.n_k1 = [P[3][:, 0:1024], P[3][:, 1024:2048]]
    C.n_k3T = [P[15][:, 0:1024], P[15][:, 1024:2048]]
    C.n_k3 = P[16][0:64, :]
    C.n_k1T = P[17][0:64, :]
    C.n_lmf = [sb(f"n_lmf{i}", [128, 128], F32) for i in range(2)]
    C.n_lmb = [sb(f"n_lmb{i}", [128, 128], BF16) for i in range(2)]
    C.n_s = [sb(f"n_s{i}", [64, 11 * 64], BF16) for i in range(2)]
    C.n_cI = sb("n_cI", [64, 192], BF16)
    C.n_k1sum = [sb(f"n_k1sum{i}", [128, 16], F32) for i in range(2)]
    C.n_vsmall = [sb(f"n_vsmall{i}", [128, 4], F32) for i in range(2)]


def _sl64(t, i):
    return t[:, i * 64:(i + 1) * 64]


def l1_attn_init(fw, C):
    for i, v in enumerate((7.0, 15.0, 13.0)):
        fw.ts(_sl64(C.n_cI, i), C.eye_bf[0:64, 0:64], v, None, ALU.mult)


def l1_attn(fw, C, W, seq_tag=""):
    layernorm(fw, C, tag=seq_tag + "L1a.")

    for pair in range(8):
        fw.phase = seq_tag + "l1.qkv"
        for col0, slot in ((0, C.wsQ), (1024, C.wsK), (2048, C.wsV)):
            _stream_w_chunk(fw, C, slot, W["w_qkv"], col0 + pair * 128, 128,
                            tag=f"{seq_tag}l1w{pair}.{col0}")
        for (slot, dst, scale_ap) in ((C.wsQ, C.qb, cst(C, C.QSCALE)),
                                      (C.wsK, C.kb, None), (C.wsV, C.vb, None)):
            def evac(h, ph, dst=dst, scale_ap=scale_ap):
                if scale_ap is not None:
                    fw.act(_halves(dst)[h], ph[:, :], AF.Identity,
                           scale=scale_ap, tag=f"{seq_tag}l1ev{pair}.{h}")
                else:
                    fw.act(_halves(dst)[h], ph[:, :], AF.Copy,
                           tag=f"{seq_tag}l1ev{pair}.{h}")
            _proj_1024(fw, C, slot, evac, C.hT)

        fw.phase = seq_tag + "l1.lm"
        for h in (0, 1):
            hs = slice(64 * h, 64 * h + 64)
            fw.red(C.n_lmf[h][hs, 0:64],
                   C.kb[hs, :].rearrange("p (m w) -> p m w", w=32), ALU.add)
            fw.act(C.n_lmb[h][hs, 0:64], C.n_lmf[h][hs, 0:64], AF.Identity,
                   scale=cst(C, C.INV32)[hs, :])
            fw.red(C.n_lmf[h][hs, 64:128],
                   C.qb[hs, :].rearrange("p (m w) -> p m w", w=32), ALU.add)
            fw.act(C.n_lmb[h][hs, 64:128], C.n_lmf[h][hs, 64:128], AF.Identity,
                   scale=cst(C, C.INV32)[hs, :])
        for h in (0, 1):
            fw.phase = seq_tag + "l1.head"
            hs = slice(64 * h, 64 * h + 64)
            vtm = C.vtm[h]
            S = C.n_s[h]
            K2T, KVb, KVT, Aw, Cw, Ew, Vb, VT, M1b, M2b, K2b = range(11)
            ident = C.eye_bf[hs, hs]
            ident0 = C.eye_bf[0:64, 0:64]
            k1 = C.n_k1[h]
            k3 = C.n_k3
            k3T = C.n_k3T[h]
            k1T = C.n_k1T
            vsm = C.n_vsmall[h]
            _vtm_ones(fw, C, h)
            for kc in range(16):
                _tc(fw, C, vtm[:, kc * 65:kc * 65 + 64],
                    C.vb[hs, kc * 128:(kc + 1) * 128], ident, kc, 64, 128,
                    tag=f"{seq_tag}nvtm")
            klm = C.n_lmb[h][hs, 0:64]
            qlm = C.n_lmb[h][hs, 64:128]
            for tc in range(16):
                ah = C.Ah[tc // 8]
                ofs = (tc % 8) * 128
                fw.mm(ah[:, ofs:ofs + 64], C.qb[hs, tc * 128:(tc + 1) * 128],
                      klm, start=True, stop=True)
            for tc in range(16):
                ah = C.Ah[tc // 8]
                ofs = (tc % 8) * 128
                fw.act(k1[:, tc * 64:(tc + 1) * 64], ah[:, ofs:ofs + 64],
                       AF.Exp, accum_out=C.n_k1sum[h][:, tc:tc + 1],
                       tag=f"{seq_tag}k1e{tc}")
            fw.recip(C.n_k1sum[h][:, :], C.n_k1sum[h][:, :])
            for tc in range(16):
                fw.ts(k1[:, tc * 64:(tc + 1) * 64], k1[:, tc * 64:(tc + 1) * 64],
                      C.n_k1sum[h][:, tc:tc + 1], None, ALU.mult)
            for tc in range(16):
                _tc(fw, C, k1T[0:64, tc * 128:(tc + 1) * 128],
                    k1[:, tc * 64:(tc + 1) * 64], C.eye_bf[:, :], tc, 128, 64,
                    tag=f"{seq_tag}k1T{tc}")
            fw.mm(C.BL[0:64, 0:64], qlm, klm, start=True, stop=True)
            fw.act(_sl64(S, K2b)[0:64, :], C.BL[0:64, 0:64], AF.Exp,
                   accum_out=vsm[0:64, 1:2])
            fw.recip(vsm[0:64, 1:2], vsm[0:64, 1:2])
            fw.ts(_sl64(S, K2b)[0:64, :], _sl64(S, K2b)[0:64, :],
                  vsm[0:64, 1:2], None, ALU.mult)
            for n in range(4):
                sl = slice(n * 512, (n + 1) * 512)
                fw.mm(C.Bn[n][0:64, :], qlm, C.kb[hs, sl], start=True, stop=True)
            for hh in range(2):
                fw.act(k3[0:64, hh * 1024:(hh + 1) * 1024], C.Bh[hh][0:64, :], AF.Exp,
                       accum_out=vsm[0:64, 2 + hh:3 + hh], tag=f"{seq_tag}k3e{hh}")
            fw.tt(vsm[0:64, 0:1], vsm[0:64, 2:3], vsm[0:64, 3:4], ALU.add)
            fw.recip(vsm[0:64, 0:1], vsm[0:64, 0:1])
            fw.ts(k3[0:64, :], k3[0:64, :], vsm[0:64, 0:1], None, ALU.mult)
            for tc in range(16):
                _tc(fw, C, k3T[:, tc * 64:(tc + 1) * 64],
                    k3[0:64, tc * 128:(tc + 1) * 128], ident0, tc, 64, 128,
                    tag=f"{seq_tag}k3T{tc}")
            fw.phase = seq_tag + "l1.newton"
            fw.trans(C.PTbB[0:64, 64:128], _sl64(S, K2b)[0:64, :], ident0)
            fw.copy(_sl64(S, K2T)[0:64, :], C.PTbB[0:64, 64:128])
            fw.mm(C.BL[0:1, 0:64], C.allones[0:64, 0:1], _sl64(S, K2b)[0:64, :],
                  start=True, stop=True)
            fw.red(vsm[0:1, 2:3], C.BL[0:1, 0:64], ALU.max)
            fw.recip(vsm[0:1, 2:3], vsm[0:1, 2:3])
            fw.act(C.row_bf[0:1, 0:1], vsm[0:1, 2:3], AF.Copy)
            fw.mm(C.BL[0:64, 64:65], C.ones1[0:1, 0:64], C.row_bf[0:1, 0:1],
                  start=True, stop=True)
            fw.copy(vsm[0:64, 3:4], C.BL[0:64, 64:65])
            fw.ts(_sl64(S, Vb)[0:64, :], _sl64(S, K2T)[0:64, :],
                  vsm[0:64, 3:4], None, ALU.mult)
            for it in range(6):
                fw.mm(C.BL[0:64, 0:64], _sl64(S, K2T)[0:64, :], _sl64(S, Vb)[0:64, :],
                      start=True, stop=True)
                fw.copy(_sl64(S, KVb)[0:64, :], C.BL[0:64, 0:64])
                fw.tt(_sl64(S, Aw)[0:64, :], _sl64(C.n_cI, 0)[0:64, :],
                      C.BL[0:64, 0:64], ALU.subtract)
                fw.trans(C.PTbB[0:64, 256:320], _sl64(S, KVb)[0:64, :], ident0)
                fw.copy(_sl64(S, KVT)[0:64, :], C.PTbB[0:64, 256:320])
                fw.mm(C.BL[0:64, 0:64], _sl64(S, KVT)[0:64, :], _sl64(S, Aw)[0:64, :],
                      start=True, stop=True)
                fw.tt(_sl64(S, Cw)[0:64, :], _sl64(C.n_cI, 1)[0:64, :],
                      C.BL[0:64, 0:64], ALU.subtract)
                fw.mm(C.BL[0:64, 0:64], _sl64(S, KVT)[0:64, :], _sl64(S, Cw)[0:64, :],
                      start=True, stop=True)
                fw.tt(_sl64(S, Ew)[0:64, :], _sl64(C.n_cI, 2)[0:64, :],
                      C.BL[0:64, 0:64], ALU.subtract)
                fw.trans(C.PTbB[0:64, 256:320], _sl64(S, Vb)[0:64, :], ident0)
                fw.copy(_sl64(S, VT)[0:64, :], C.PTbB[0:64, 256:320])
                fw.mm(C.BL[0:64, 0:64], _sl64(S, VT)[0:64, :], _sl64(S, Ew)[0:64, :],
                      start=True, stop=True)
                fw.act(_sl64(S, Vb)[0:64, :], C.BL[0:64, 0:64], AF.Identity,
                       scale=cst(C, 6)[0:64, :], tag=f"{seq_tag}nwt{it}")
            fw.phase = seq_tag + "l1.M"
            for kc in range(16):
                fw.mm(C.BL[0:64, 64:128], k3T[:, kc * 64:(kc + 1) * 64],
                      vtm[:, kc * 65:kc * 65 + 64], start=(kc == 0), stop=(kc == 15))
            fw.copy(_sl64(S, M1b)[0:64, :], C.BL[0:64, 64:128])
            fw.trans(C.PTbB[0:64, 256:320], _sl64(S, Vb)[0:64, :], ident0)
            fw.copy(_sl64(S, VT)[0:64, :], C.PTbB[0:64, 256:320])
            fw.mm(C.BL[0:64, 0:64], _sl64(S, VT)[0:64, :], _sl64(S, M1b)[0:64, :],
                  start=True, stop=True)
            fw.copy(_sl64(S, M2b)[0:64, :], C.BL[0:64, 0:64])
            for n in range(4):
                sl = slice(n * 512, (n + 1) * 512)
                fw.mm(C.Bn[n][0:64, :], _sl64(S, M2b)[0:64, :], k1T[0:64, sl],
                      start=True, stop=True)
            for hh in range(2):
                fw.act(C.svT[pair][hs, hh * 1024:(hh + 1) * 1024], C.Bh[hh][0:64, :],
                       AF.Copy, tag=f"{seq_tag}svl1{pair}.{h}.{hh}")

    for o in range(8):
        fw.phase = seq_tag + "l1.oproj"
        slot = C.wsQ if o % 2 == 0 else C.wsO
        _stream_w_chunk(fw, C, slot, W["w_proj"], o * 128, 128, tag=f"{seq_tag}l1wo{o}")

        def evac(h, ph, o=o):
            hsl = slice(h * 1024, (h + 1) * 1024)
            fw.tt(C.xT[o][:, hsl], C.xT[o][:, hsl], ph[:, :], ALU.add,
                  tag=f"{seq_tag}l1x{o}.{h}")
        _proj_1024(fw, C, slot, evac, C.svT)



def alloc_l1_ffn(nc, stack, fw, C):
    P = C.pool
    C.f_dec = [P[4], P[5], P[6], P[7]]
    C.f_acts = [P[8], P[9], P[10], P[11]]
    C.f_mT = [P[12], P[17], P[0], P[1]]
    C.f_ind = P[13]
    C.f_win = [P[14], P[15]]
    C.f_wout = P[16]
    C.f_binb = C.mean.bitcast(BF16)[0:1, 0:2040]
    C.f_scr = C.var


def l1_ffn(fw, C, W, seq_tag=""):
    layernorm(fw, C, tag=seq_tag + "ffn.")
    fw.dma(C.f_binb, W["b_in"], eng="pool", tag=f"{seq_tag}binb")
    P4 = (C.AL, C.AH, C.BL, C.BH)

    for tg in range(4):
        fw.phase = seq_tag + "ffn"
        tcs = tuple(4 * tg + j for j in range(4))
        for wh in range(2):
            w0 = wh * 1024
            wn = 1024 if wh == 0 else 1016
            for kc in range(8):
                slot = C.f_win[kc % 2]
                fw.dma(slot[:, 0:wn], W["w_inT"][kc * 128:(kc + 1) * 128, w0:w0 + wn],
                       eng="pool", tag=f"{seq_tag}win{tg}.{wh}.{kc}")
                for j, tc in enumerate(tcs):
                    lhsT = C.hT[kc][:, tc * 128:(tc + 1) * 128]
                    for (a, b) in ((0, 512), (512, wn)):
                        if kc == 0:
                            fw.mm(P4[j][:, a:b], C.ones1[0:1, :],
                                  C.f_binb[0:1, w0 + a:w0 + b], start=True, stop=False)
                        fw.mm(P4[j][:, a:b], lhsT, slot[:, a:b],
                              start=False, stop=(kc == 7))
            for j, tc in enumerate(tcs):
                fw.ts(C.f_dec[j][:, w0:w0 + wn], P4[j][:, 0:wn], 0.0, None, ALU.is_gt)
                fw.act(C.f_acts[j][:, w0:w0 + wn], P4[j][:, 0:wn], AF.Silu,
                       tag=f"{seq_tag}fsilu{tc}.{wh}")
        for j, tc in enumerate(tcs):
            ind3 = C.f_ind[:, 0:2040].rearrange("p (r n) -> p r n", n=255)
            dec3 = C.f_dec[j][:, 0:2040].rearrange("p (r n) -> p r n", n=255)
            fw.memset(ind3[:, :, 0:1], 1.0)
            for d in range(7):
                p0 = 2 ** d - 1
                n = 2 ** d
                c0 = 2 ** (d + 1) - 1
                par = ind3[:, :, p0:p0 + n]
                dd = dec3[:, :, p0:p0 + n]
                right = ind3[:, :, c0 + 1:c0 + 2 * n:2]
                left = ind3[:, :, c0:c0 + 2 * n:2]
                fw.tt(right, par, dd, ALU.mult)
                fw.tt(left, par, right, ALU.subtract)
            fw.tt(C.f_acts[j][:, 0:2040], C.f_acts[j][:, 0:2040],
                  C.f_ind[:, 0:2040], ALU.mult, tag=f"{seq_tag}fmask{tc}")
            for kc in range(16):
                wn2 = 2040 - kc * 128 if kc == 15 else 128
                _tc(fw, C, C.f_mT[j][0:wn2, kc * 128:(kc + 1) * 128],
                    C.f_acts[j][:, kc * 128:kc * 128 + wn2], C.eye_bf[:, :],
                    kc + j, 128, wn2, tag=f"{seq_tag}fmT{tc}.{kc}")
        for kc in range(16):
            wn2 = 2040 - kc * 128 if kc == 15 else 128
            half = C.f_wout[:, (kc % 2) * 1024:(kc % 2 + 1) * 1024]
            fw.dma(half[0:wn2, :], W["w_outT"][kc * 128:kc * 128 + wn2, :],
                   eng="pool", tag=f"{seq_tag}wo{tg}.{kc}")
            for j, tc in enumerate(tcs):
                lhsT = C.f_mT[j][0:wn2, kc * 128:(kc + 1) * 128]
                for n in range(2):
                    fw.mm(P4[j][:, n * 512:(n + 1) * 512], lhsT,
                          half[0:wn2, n * 512:(n + 1) * 512],
                          start=(kc == 0), stop=(kc == 15))
        for j, tc in enumerate(tcs):
            fw.act(C.f_scr[:, 0:1024], P4[j][:, :], AF.Copy, tag=f"{seq_tag}fev{tc}")
            for o in range(8):
                bb = P4[j]
                fw.trans(bb[:, 0:128], C.f_scr[:, o * 128:(o + 1) * 128],
                         C.eye_f32[:, :])
                fw.tt(C.xT[o][:, tc * 128:(tc + 1) * 128],
                      C.xT[o][:, tc * 128:(tc + 1) * 128], bb[:, 0:128], ALU.add,
                      tag=f"{seq_tag}fx{tc}.{o}")



NCORES = 8
NSEQ = 4


def _host_consts():
    return {
        "c_allones": np.ones((128, 128), np.float32),
        "c_ones1": np.ones((1, 128), np.float32),
        "c_eye": np.eye(128, dtype=np.float32),
        "c_cst": np.broadcast_to(
            np.array([1e-5, -0.5, 0.125, 2.5, 1e-20, 1.0 / 32, 0.25, 0], np.float32),
            (128, 8)).copy(),
    }


CONST_SHAPES = {"c_allones": [128, 128], "c_ones1": [1, 128],
                "c_eye": [128, 128], "c_cst": [128, 8]}

L0A_SHAPES = {"wqT": [1024, 1024], "wkT": [1024, 1024], "wvT": [1024, 1024],
              "woT": [1024, 1024], "b_l0": [128, 32]}
MOE_SHAPES = {"gwT": [1024, 32], "gb128": [128, 32],
              "wgT": [32, 1024, 512], "wuT": [32, 1024, 512], "wdT": [32, 512, 1024],
              "shg2": [2, 1024, 512], "shu2": [2, 1024, 512], "shd2": [2, 512, 1024]}
L1A_SHAPES = {"w_qkv": [1024, 3072], "w_proj": [1024, 1024]}
L1F_SHAPES = {"w_inT": [1024, 2040], "b_in": [1, 2040], "w_outT": [2040, 1024]}


def _prep_l0a(W_qkv, b_qkv, W_out, b_out):
    qr = np.concatenate([np.arange(h * 192, h * 192 + 64) for h in range(16)])
    out = {
        "wqT": np.ascontiguousarray(W_qkv[qr, :].T),
        "wkT": np.ascontiguousarray(W_qkv[qr + 64, :].T),
        "wvT": np.ascontiguousarray(W_qkv[qr + 128, :].T),
        "woT": np.ascontiguousarray(W_out.T),
    }
    bq = b_qkv[qr].reshape(8, 128).T
    bk = b_qkv[qr + 64].reshape(8, 128).T
    bv = b_qkv[qr + 128].reshape(8, 128).T
    bo = b_out.reshape(8, 128).T
    out["b_l0"] = np.ascontiguousarray(
        np.concatenate([bq, bk, bv, bo], axis=1)).astype(np.float32)
    return out


def _prep_moe(gate_w, gate_bias, w_gate, w_up, w_down, sh_gate, sh_up, sh_down):
    shgT, shuT, shdT = sh_gate.T, sh_up.T, sh_down.T
    return {
        "gwT": np.ascontiguousarray(gate_w.T),
        "gb128": np.broadcast_to(gate_bias[None, :], (128, 32)).astype(np.float32).copy(),
        "wgT": np.ascontiguousarray(w_gate.transpose(0, 2, 1)),
        "wuT": np.ascontiguousarray(w_up.transpose(0, 2, 1)),
        "wdT": np.ascontiguousarray(w_down.transpose(0, 2, 1)),
        "shg2": np.ascontiguousarray(np.stack([shgT[:, 0:512], shgT[:, 512:1024]])),
        "shu2": np.ascontiguousarray(np.stack([shuT[:, 0:512], shuT[:, 512:1024]])),
        "shd2": np.ascontiguousarray(np.stack([shdT[0:512, :], shdT[512:1024, :]])),
    }


def _declare(nc, shapes):
    d = {}
    for n, shp in shapes.items():
        d[n] = nc.dram_tensor(n, list(shp), F32, kind="ExternalInput").ap()
    return d


def _build_block(block):
    nc = bass.Bass("TRN2", target_bir_lowering=False)
    x_d = nc.dram_tensor("xT", [NSEQ, 1024, 2048], F32, kind="ExternalInput").ap()
    y_d = nc.dram_tensor("yT", [NSEQ, 1024, 2048], F32, kind="ExternalOutput").ap()
    dram = _declare(nc, CONST_SHAPES)
    wshapes = {"l0a": L0A_SHAPES, "moe": MOE_SHAPES,
               "l1a": L1A_SHAPES, "l1f": L1F_SHAPES}[block]
    W = _declare(nc, wshapes)

    import contextlib
    with contextlib.ExitStack() as stack:
        fw = FW(nc)
        C = alloc_common(nc, stack, fw)
        if block == "l0a":
            alloc_l0_attn(nc, stack, fw, C)
        elif block == "moe":
            alloc_l0_moe(nc, stack, fw, C)
        elif block == "l1a":
            alloc_l0_attn(nc, stack, fw, C)
            alloc_l1_attn(nc, stack, fw, C)
        else:
            alloc_l1_ffn(nc, stack, fw, C)
        emit_const_init(fw, C, dram)
        if block == "l0a":
            l0_attn_init(fw, C, W)
        elif block == "moe":
            l0_moe_init(fw, C, W)
        elif block == "l1a":
            for i in range(2):
                ap = C.vtm[i].ap().rearrange("p (a n) -> p a n", n=65)
                fw.memset(ap[:, :, 64:65], 1.0)
            l1_attn_init(fw, C)
        for s in range(NSEQ):
            load_xT(fw, C, x_d, s)
            if block == "l0a":
                l0_attn(fw, C, W, seq_tag=f"s{s}.")
            elif block == "moe":
                l0_moe(fw, C, W, seq_tag=f"s{s}.")
            elif block == "l1a":
                l1_attn(fw, C, W, seq_tag=f"s{s}.")
            else:
                l1_ffn(fw, C, W, seq_tag=f"s{s}.")
            store_xT(fw, C, y_d, s)
        fw.finalize(stack)
        fw.emit()
    return nc


def _build_fused():
    nc = bass.Bass("TRN2", target_bir_lowering=False)
    x_d = nc.dram_tensor("xT", [NSEQ, 1024, 2048], F32, kind="ExternalInput").ap()
    y_d = nc.dram_tensor("yT", [NSEQ, 1024, 2048], F32, kind="ExternalOutput").ap()
    dram = _declare(nc, CONST_SHAPES)
    W = {}
    for shapes in (L0A_SHAPES, MOE_SHAPES, L1A_SHAPES, L1F_SHAPES):
        W.update(_declare(nc, shapes))

    import contextlib
    with contextlib.ExitStack() as stack:
        fw = FW(nc)
        C = alloc_common(nc, stack, fw)
        alloc_l0_attn(nc, stack, fw, C)
        alloc_l0_moe(nc, stack, fw, C)
        alloc_l1_attn(nc, stack, fw, C)
        alloc_l1_ffn(nc, stack, fw, C)
        emit_const_init(fw, C, dram)
        l0_attn_init(fw, C, W)
        l0_moe_init(fw, C, W)
        l1_attn_init(fw, C)
        for s in range(NSEQ):
            load_xT(fw, C, x_d, s)
            l0_attn(fw, C, W, seq_tag=f"s{s}a.")
            l0_moe(fw, C, W, seq_tag=f"s{s}m.")
            l1_attn(fw, C, W, seq_tag=f"s{s}n.")
            l1_ffn(fw, C, W, seq_tag=f"s{s}f.")
            store_xT(fw, C, y_d, s)
        fw.finalize(stack)
        fw.emit()
    _flip_ldweights(nc, fw.flip_names)
    return nc


def _flip_ldweights(nc, names):
    import copy as _c
    todo = set(names)
    if not todo:
        return
    for blk in nc.main_func.blocks:
        il = blk.instructions
        hit = False
        for i, ins in enumerate(il):
            if ins.name in todo:
                il[i] = _c.replace(ins, ldweights=False)
                todo.discard(ins.name)
                hit = True
        if hit and not todo:
            break
    assert not todo, f"{len(todo)} skip-ldweights matmuls not found in BIR"


def kernel(L0_attn_x, L0_attn_W_qkv, L0_attn_b_qkv, L0_attn_W_out, L0_attn_b_out,
           L0_ffn_gate_w, L0_ffn_gate_bias, L0_ffn_w_gate, L0_ffn_w_up,
           L0_ffn_w_down, L0_ffn_sh_gate, L0_ffn_sh_up, L0_ffn_sh_down,
           L1_attn_w_qkv, L1_attn_w_proj, L1_ffn_W_in, L1_ffn_b_in, L1_ffn_W_out):
    f32 = lambda a: np.asarray(a, dtype=np.float32)
    x = f32(L0_attn_x)
    xT = np.ascontiguousarray(x.transpose(0, 2, 1))

    consts = _host_consts()
    w_l0a = _prep_l0a(f32(L0_attn_W_qkv), f32(L0_attn_b_qkv),
                      f32(L0_attn_W_out), f32(L0_attn_b_out))
    w_moe = _prep_moe(f32(L0_ffn_gate_w), f32(L0_ffn_gate_bias), f32(L0_ffn_w_gate),
                      f32(L0_ffn_w_up), f32(L0_ffn_w_down), f32(L0_ffn_sh_gate),
                      f32(L0_ffn_sh_up), f32(L0_ffn_sh_down))
    w_l1a = {"w_qkv": f32(L1_attn_w_qkv), "w_proj": f32(L1_attn_w_proj)}
    w_l1f = {"w_inT": np.ascontiguousarray(f32(L1_ffn_W_in).T),
             "b_in": f32(L1_ffn_b_in).reshape(1, 2040),
             "w_outT": np.ascontiguousarray(f32(L1_ffn_W_out).T)}

    shards = [np.ascontiguousarray(xT[i * NSEQ:(i + 1) * NSEQ]) for i in range(NCORES)]
    nc = _build_fused()
    in_maps = []
    for i in range(NCORES):
        m = {"xT": shards[i]}
        m.update(consts)
        m.update(w_l0a)
        m.update(w_moe)
        m.update(w_l1a)
        m.update(w_l1f)
        in_maps.append(m)
    res = run_bass_kernel_spmd(nc, in_maps, core_ids=list(range(NCORES)))
    shards = [np.asarray(r["yT"]) for r in res.results]

    yT = np.concatenate(shards, axis=0)
    return np.ascontiguousarray(yT.transpose(0, 2, 1))



# version 13 (final)
# speedup vs baseline: 1.1046x; 1.0275x over previous
import sys
sys.path.insert(0, "/opt/trn_rl_repo")

import numpy as np
import concourse.bass as bass
import concourse.mybir as mybir
from concourse.bass_utils import run_bass_kernel_spmd




F32 = mybir.dt.float32
BF16 = mybir.dt.bfloat16
AF = mybir.ActivationFunctionType
ALU = mybir.AluOpType
AX = mybir.AxisListType

ENGINES = ("pe", "act", "dve", "pool", "sp")
DMAQ = {"sp": "spq", "pool": "poolq", "act": "actq"}


def _ap_key(ap):
    t = getattr(ap, "tensor", None)
    if t is None:
        return ap.name
    return t.name


def _is_psum(ap):
    t = getattr(ap, "tensor", ap)
    sp = getattr(t, "space", None)
    return sp is not None and str(getattr(sp, "value", sp)) == "psum"


class _Op:
    __slots__ = ("i", "eng", "fn", "reads", "writes", "dma", "tag",
                 "deps", "token", "waits", "vc", "psum_reads", "pre_wait",
                 "phase", "bir", "skip_ldw", "lw_sig", "lw_key")

    def __init__(self, i, eng, fn, reads, writes, dma, tag):
        self.i = i
        self.eng = eng
        self.fn = fn
        self.reads = reads
        self.writes = writes
        self.dma = dma
        self.tag = tag
        self.deps = []
        self.token = None
        self.waits = []
        self.vc = None
        self.skip_ldw = False
        self.lw_sig = None
        self.lw_key = None


def _lw_sig(ap):
    try:
        return (ap.name, ap.offset, tuple(ap.shape), str(ap.stride()),
                ap.base_partition(), str(ap.dtype))
    except Exception:
        return None


class FW:
    def __init__(self, nc):
        self.nc = nc
        self.phase = ""
        self.ops = []
        self._lw = {}
        self._rd = {}
        self.sems = {}
        self.max_tok = {}

    def _rec(self, eng, fn, reads, writes, dma=False, tag=""):
        rl = [a for a in reads if a is not None and not isinstance(a, (int, float))]
        rk = [_ap_key(a) for a in rl]
        wk = [_ap_key(a) for a in writes]
        op = _Op(len(self.ops), eng, fn, rk, wk, dma, tag)
        op.phase = self.phase
        op.psum_reads = [_ap_key(a) for a in rl if _is_psum(a)]
        self.ops.append(op)
        return op

    def mm(self, out, lhsT, rhs, start=True, stop=True, tag=""):
        op = self._rec("pe", lambda e: e.matmul(out, lhsT, rhs, start=start, stop=stop),
                       [lhsT, rhs], [out], tag=tag)
        op.lw_sig = _lw_sig(lhsT)
        op.lw_key = _ap_key(lhsT)

    def trans(self, out, in_, ident, tag=""):
        self._rec("pe", lambda e: e.transpose(out, in_, ident), [in_, ident], [out], tag=tag)

    def act(self, out, in_, func, bias=0.0, scale=1.0, accum_out=None, tag=""):
        def f(e):
            kw = {}
            if accum_out is not None:
                kw["accum_out"] = accum_out
            return e.activation(out, in_, func, bias=bias, scale=scale, **kw)
        wr = [out] if accum_out is None else [out, accum_out]
        self._rec("act", f, [in_, bias if not isinstance(bias, float) else None,
                             scale if not isinstance(scale, float) else None], wr, tag=tag)

    def eng_op(self, eng, fn, reads, writes, tag=""):
        self._rec(eng, fn, reads, writes, tag=tag)

    def tt(self, out, a, b, op, eng="dve", tag=""):
        self._rec(eng, lambda e: e.tensor_tensor(out, a, b, op), [a, b], [out], tag=tag)

    def ts(self, out, in_, s1, s2, op0, op1=ALU.bypass, eng="dve", tag=""):
        self._rec(eng, lambda e: e.tensor_scalar(out, in_, s1, s2, op0, op1),
                  [in_, s1 if not isinstance(s1, (int, float)) else None,
                   s2 if not isinstance(s2, (int, float)) else None], [out], tag=tag)

    def tss(self, out, in_, s, op, eng="dve", tag=""):
        self._rec(eng, lambda e: e.tensor_single_scalar(out, in_, s, op),
                  [in_, s if not isinstance(s, (int, float)) else None], [out], tag=tag)

    def stt(self, out, in0, s, in1, op0, op1, eng="dve", tag=""):
        self._rec(eng, lambda e: e.scalar_tensor_tensor(out, in0, s, in1, op0, op1),
                  [in0, in1, s if not isinstance(s, (int, float)) else None], [out], tag=tag)

    def red(self, out, in_, op, negate=False, eng="dve", tag=""):
        self._rec(eng, lambda e: e.tensor_reduce(out, in_, AX.X, op, negate=negate),
                  [in_], [out], tag=tag)

    def copy(self, out, in_, eng="dve", tag=""):
        self._rec(eng, lambda e: e.tensor_copy(out, in_), [in_], [out], tag=tag)

    def recip(self, out, in_, tag=""):
        self._rec("dve", lambda e: e.reciprocal(out, in_), [in_], [out], tag=tag)

    def memset(self, ap, v, eng="dve", tag=""):
        self._rec(eng, lambda e: e.memset(ap, v), [], [ap], tag=tag)

    def dma(self, out, in_, eng="sp", tag=""):
        self._rec(eng, lambda e: e.dma_start(out, in_), [in_], [out], dma=True, tag=tag)

    def barrier_note(self):
        pass

    def finalize(self, stack):
        ops = self.ops
        lw = {}
        rd = {}
        psum_rd = {}
        pe_sig = pe_ver = None
        for op in ops:
            dep = set()
            for k in op.reads:
                if k in lw:
                    dep.add(lw[k])
            for k in op.psum_reads:
                for (r, e) in psum_rd.get(k, ()):
                    if e != op.eng:
                        dep.add(r)
            for k in op.writes:
                if k in lw:
                    dep.add(lw[k])
                for r in rd.get(k, ()):
                    dep.add(r)
            dep.discard(op.i)
            last_per_eng = {}
            dma_deps = set()
            for d in dep:
                dop = ops[d]
                if dop.dma:
                    dma_deps.add(d)
                    continue
                if dop.eng == "pe" and op.eng == "pe" and not op.dma:
                    continue
                if d > last_per_eng.get(dop.eng, -1):
                    last_per_eng[dop.eng] = d
            op.deps = sorted(dma_deps | set(last_per_eng.values()))
            if op.eng == "pe":
                if op.lw_sig is not None:
                    ver = lw.get(op.lw_key, -1)
                    if pe_sig is not None and op.lw_sig == pe_sig and ver == pe_ver:
                        op.skip_ldw = True
                    pe_sig, pe_ver = op.lw_sig, ver
                else:
                    pe_sig = pe_ver = None
            for k in op.reads:
                rd.setdefault(k, []).append(op.i)
            for k in op.psum_reads:
                psum_rd.setdefault(k, []).append((op.i, op.eng))
            for k in op.writes:
                lw[k] = op.i
                rd[k] = []
                psum_rd[k] = []

        needed = set()
        for op in ops:
            for d in op.deps:
                needed.add(d)
        NSEM = 16
        counters = {}
        qcount = {}
        for op in ops:
            if op.dma:
                q = DMAQ[op.eng]
                i = qcount.get(q, 0)
                qcount[q] = i + 1
                stream = f"{q}{i % NSEM}"
                epoch = i // NSEM
                op.token = (stream, 16 * (epoch + 1))
                counters[stream] = 16 * (epoch + 1)
                op.pre_wait = (stream, 16 * epoch) if epoch > 0 else None
            else:
                op.pre_wait = None
                if op.i in needed:
                    stream = op.eng
                    counters[stream] = counters.get(stream, 0) + 1
                    op.token = (stream, counters[stream])
        self.max_tok = dict(counters)

        evc = {e: {} for e in ENGINES}
        svc = {e: {} for e in ENGINES}
        for op in ops:
            know = evc[op.eng]
            waits = []
            mvc = dict(svc[op.eng])
            for d in op.deps:
                dop = ops[d]
                if dop.token is None:
                    continue
                s, v = dop.token
                if know.get(s, 0) < v:
                    waits.append((s, v))
                    know[s] = v
                    for ks, kv in dop.vc.items():
                        if know.get(ks, 0) < kv:
                            know[ks] = kv
                for ks, kv in dop.vc.items():
                    if mvc.get(ks, 0) < kv:
                        mvc[ks] = kv
            if op.pre_wait is not None:
                s, v = op.pre_wait
                if know.get(s, 0) < v:
                    waits.append((s, v))
                    know[s] = v
            wmax = {}
            for s, v in waits:
                wmax[s] = max(wmax.get(s, 0), v)
            op.waits = sorted(wmax.items())
            svc_update = dict(mvc)
            if op.token is not None:
                s, v = op.token
                mvc = dict(mvc)
                mvc[s] = v
                if not op.dma:
                    svc_update[s] = v
            op.vc = mvc
            svc[op.eng] = svc_update

        streams = set()
        for op in ops:
            if op.token:
                streams.add(op.token[0])
        for s in sorted(streams):
            self.sems[s] = stack.enter_context(self.nc.semaphore(f"s_{s}"))

        return self

    def emit(self, final_waits=True):
        nc = self.nc
        sems = self.sems
        by_eng = {e: [] for e in ENGINES}
        for op in self.ops:
            by_eng[op.eng].append(op)
        self.flip_names = []

        def body(eng_name):
            def run(e):
                for op in by_eng[eng_name]:
                    for s, v in op.waits:
                        e.wait_ge(sems[s], v)
                    inst = op.fn(e)
                    op.bir = inst.ins.name
                    if op.skip_ldw:
                        self.flip_names.append(op.bir)
                    if op.token is not None:
                        s, v = op.token
                        inst.then_inc(sems[s], 16 if op.dma else 1)
                if eng_name == "sp" and final_waits:
                    for s, v in self.max_tok.items():
                        if s in sems:
                            e.wait_ge(sems[s], v)
            return run

        with nc.Block() as block:
            block.tensor(body("pe"))
            block.scalar(body("act"))
            block.vector(body("dve"))
            block.gpsimd(body("pool"))
            block.sync(body("sp"))

    def stats(self):
        n = {e: 0 for e in ENGINES}
        w = 0
        t = 0
        for op in self.ops:
            n[op.eng] += 1
            w += len(op.waits)
            t += 1 if op.token else 0
        return {"ops": len(self.ops), "per_engine": n, "waits": w, "tokens": t}





class Ctx:
    pass


NPOOL = 18


def alloc_common(nc, stack, fw):
    C = Ctx()
    sb = lambda name, shape, dt: stack.enter_context(nc.sbuf_tensor(name, shape, dt))
    ps = lambda name, shape, dt: stack.enter_context(nc.psum_tensor(name, shape, dt))

    C.xT = [sb(f"xT{c}", [128, 2048], F32) for c in range(8)]
    C.hT = [sb(f"hT{c}", [128, 2048], BF16) for c in range(8)]
    C.pool = [sb(f"pl{i}", [128, 2048], BF16) for i in range(NPOOL)]

    C.AL = ps("psAL", [128, 1024], F32)
    C.AH = ps("psAH", [128, 1024], F32)
    C.BL = ps("psBL", [128, 1024], F32)
    C.BH = ps("psBH", [128, 1024], F32)
    C.An = [C.AL[:, 0:512], C.AL[:, 512:1024], C.AH[:, 0:512], C.AH[:, 512:1024]]
    C.Bn = [C.BL[:, 0:512], C.BL[:, 512:1024], C.BH[:, 0:512], C.BH[:, 512:1024]]
    C.Ah = [C.AL, C.AH]
    C.Bh = [C.BL, C.BH]
    C.PTbB = C.BL.bitcast(BF16)
    C.PTb4 = [C.AL.bitcast(BF16), C.AH.bitcast(BF16),
              C.BL.bitcast(BF16), C.BH.bitcast(BF16)]
    C.grp = 0

    C.mean = sb("ln_mean", [128, 2048], F32)
    C.var = sb("ln_var", [128, 2048], F32)
    C.xb = [sb(f"ln_xb{i}", [128, 2048], BF16) for i in range(2)]

    C.scores_sb = C.mean[0:32, :]
    C.combT = C.var[0:32, :]
    C.row_f32 = C.var[64:65, :]
    C.row_bf = C.mean.bitcast(BF16)[0:1, 0:2048]

    C.allones = sb("allones", [128, 128], BF16)
    C.ones1 = sb("ones1", [1, 128], BF16)
    C.eye_bf = sb("eye_bf", [128, 128], BF16)
    C.eye_f32 = sb("eye_f32", [128, 128], F32)
    C.cst = sb("cst", [128, 8], F32)
    C.EPS, C.NEGHALF, C.QSCALE, C.SCALE25, C.EPS20, C.INV32 = range(6)
    return C


def cst(C, i):
    return C.cst[:, i:i + 1]


def emit_const_init(fw, C, dram):
    fw.dma(C.allones[:, :], dram["c_allones"], eng="pool")
    fw.dma(C.ones1[:, :], dram["c_ones1"], eng="pool")
    fw.dma(C.eye_bf[:, :], dram["c_eye"], eng="pool")
    fw.dma(C.eye_f32[:, :], dram["c_eye"], eng="sp")
    fw.dma(C.cst[:, :], dram["c_cst"], eng="sp")


def load_xT(fw, C, x_dram, s):
    src = x_dram.rearrange("b (c p) t -> b c p t", p=128)
    for c in range(8):
        fw.dma(C.xT[c][:, :], src[s, c, :, :], eng="sp", tag=f"ldx{c}")


def store_xT(fw, C, y_dram, s):
    dst = y_dram.rearrange("b (c p) t -> b c p t", p=128)
    for c in range(8):
        fw.dma(dst[s, c, :, :], C.xT[c][:, :], eng="sp", tag=f"stx{c}")


def _halves(full):
    return (full[:, 0:1024], full[:, 1024:2048])


def layernorm(fw, C, tag=""):
    fw.phase = tag + "LN"
    """hT = LayerNorm(xT). Stats via all-ones matmul (sums broadcast on all
    partitions). h computed as x*rstd - (mean*rstd)."""
    for c in range(8):
        xb = C.xb[c % 2]
        fw.act(xb[:, :], C.xT[c][:, :], AF.Copy, tag=f"{tag}lnc{c}")
        for n in range(4):
            sl = slice(n * 512, (n + 1) * 512)
            fw.mm(C.An[n], C.allones[:, :], xb[:, sl], start=(c == 0), stop=(c == 7))
    for h in range(2):
        fw.ts(C.mean[:, h * 1024:(h + 1) * 1024], C.Ah[h][:, :], 1.0 / 1024.0, None,
              ALU.mult, tag=f"{tag}lnmean{h}")
    for c in range(8):
        xq = C.xb[c % 2]
        fw.act(xq[:, :], C.xT[c][:, :], AF.Square, tag=f"{tag}lnsq{c}")
        for n in range(4):
            sl = slice(n * 512, (n + 1) * 512)
            fw.mm(C.Bn[n], C.allones[:, :], xq[:, sl], start=(c == 0), stop=(c == 7))
    for h in range(2):
        hsl = slice(h * 1024, (h + 1) * 1024)
        fw.act(C.var[:, hsl], C.mean[:, hsl], AF.Square, tag=f"{tag}lnmsq{h}")
        fw.stt(C.var[:, hsl], C.Bh[h][:, :], 1.0 / 1024.0, C.var[:, hsl],
               ALU.mult, ALU.subtract)
        fw.act(C.var[:, hsl], C.var[:, hsl], AF.Ln, bias=cst(C, C.EPS))
        fw.act(C.var[:, hsl], C.var[:, hsl], AF.Exp, scale=cst(C, C.NEGHALF))
        fw.tt(C.mean[:, hsl], C.mean[:, hsl], C.var[:, hsl], ALU.mult)
    for c in range(8):
        eng = "dve" if c % 2 == 0 else "pool"
        fw.tt(C.hT[c][:, :], C.xT[c][:, :], C.var[:, :], ALU.mult, eng=eng)
        fw.tt(C.hT[c][:, :], C.hT[c][:, :], C.mean[:, :], ALU.subtract, eng=eng,
              tag=f"{tag}lnh{c}")


def _tc(fw, C, dst, in_, ident, k, rows, cols, rot=(0, 1, 2, 3), tag=""):
    reg = C.PTb4[rot[k % len(rot)]][0:cols, 0:rows]
    fw.trans(reg, in_, ident)
    if k % 2 == 0:
        fw.copy(dst, reg, tag=tag)
    else:
        fw.act(dst, reg, AF.Copy, tag=tag)


def _stream_w_chunk(fw, C, dst_ap, w_dram, col0, ncol, tag=""):
    src = w_dram.rearrange("(a p) n -> p a n", p=128)
    fw.dma(dst_ap.rearrange("p (a n) -> p a n", n=ncol),
           src[:, :, col0:col0 + ncol], eng="sp", tag=tag)


def _proj_1024(fw, C, slot, dest_half_ops, rhs_chunks):
    Pn, Ph = (C.An, C.Ah) if C.grp == 0 else (C.Bn, C.Bh)
    C.grp ^= 1
    for kc in range(8):
        lhsT = slot[:, kc * 128:(kc + 1) * 128]
        for n in range(4):
            sl = slice(n * 512, (n + 1) * 512)
            fw.mm(Pn[n], lhsT, rhs_chunks[kc][:, sl], start=(kc == 0), stop=(kc == 7))
    for h in range(2):
        dest_half_ops(h, Ph[h])



def alloc_l0_attn(nc, stack, fw, C):
    sb = lambda name, shape, dt: stack.enter_context(nc.sbuf_tensor(name, shape, dt))
    P = C.pool
    C.qb, C.kb, C.vb = P[0], P[1], P[2]
    C.pTx = [P[3], P[12]]
    C.svT = P[4:12]
    C.wsQ = P[13][:, 0:1024]
    C.wsK = P[13][:, 1024:2048]
    C.wsV = P[14][:, 0:1024]
    C.wsO = P[14][:, 1024:2048]
    C.vtm = [sb("vtm0", [128, 16 * 65], BF16),
             C.xb[0][:, 0:16 * 65]]
    C.bias2 = sb("bias2", [128, 32], F32)


def l0_attn_init(fw, C, W):
    fw.dma(C.bias2[:, :], W["b_l0"], eng="sp")
    fw.ts(C.bias2[:, 0:8], C.bias2[:, 0:8], 0.125, None, ALU.mult)


def _vtm_ones(fw, C, h):
    ap = C.vtm[h].rearrange("p (a n) -> p a n", n=65)
    fw.memset(ap[:, :, 64:65], 1.0)


def l0_attn(fw, C, W, seq_tag=""):
    layernorm(fw, C, tag=seq_tag + "L0a.")

    for pair in range(8):
        fw.phase = seq_tag + "l0.qkv"
        for wname, slot in (("wqT", C.wsQ), ("wkT", C.wsK), ("wvT", C.wsV)):
            _stream_w_chunk(fw, C, slot, W[wname], pair * 128, 128,
                            tag=f"{seq_tag}w{wname}{pair}")
        for (slot, dst, bofs, scale_ap) in (
            (C.wsQ, C.qb, 0, cst(C, C.QSCALE)),
            (C.wsK, C.kb, 8, None),
            (C.wsV, C.vb, 16, None),
        ):
            def evac(h, ph, dst=dst, bofs=bofs, scale_ap=scale_ap):
                kw = {"bias": C.bias2[:, bofs + pair:bofs + pair + 1]}
                if scale_ap is not None:
                    kw["scale"] = scale_ap
                fw.act(_halves(dst)[h], ph[:, :], AF.Identity,
                       tag=f"{seq_tag}ev{bofs}.{pair}", **kw)
            _proj_1024(fw, C, slot, evac, C.hT)

        for h in (0, 1):
            fw.phase = seq_tag + "l0.head"
            hs = slice(64 * h, 64 * h + 64)
            vtm = C.vtm[h]
            ident = C.eye_bf[hs, hs]
            _vtm_ones(fw, C, h)
            for kc in range(16):
                _tc(fw, C, vtm[:, kc * 65:kc * 65 + 64],
                    C.vb[hs, kc * 128:(kc + 1) * 128], ident, kc, 64, 128,
                    tag=f"{seq_tag}vtm")
            fw.phase = seq_tag + "l0.sc"
            sv = C.svT[pair]
            for qh in range(2):
                pvh = C.Bh[qh]
                pvn = (C.Bn[2 * qh], C.Bn[2 * qh + 1])
                for kc in range(17):
                    if kc < 16:
                        lhsT = C.kb[hs, kc * 128:(kc + 1) * 128]
                        sb_ = C.Ah[kc % 2]
                        pt = C.pTx[kc % 2]
                        for n in range(2):
                            fw.mm(sb_[:, n * 512:(n + 1) * 512], lhsT,
                                  C.qb[hs, qh * 1024 + n * 512:
                                       qh * 1024 + (n + 1) * 512],
                                  start=True, stop=True)
                        fw.act(pt[:, 0:1024], sb_[:, :], AF.Exp,
                               tag=f"{seq_tag}exp{pair}.{h}.{qh}.{kc}")
                    if kc >= 1:
                        j = kc - 1
                        ptj = C.pTx[j % 2]
                        for n in range(2):
                            fw.mm(pvn[n][0:65, :], vtm[:, j * 65:(j + 1) * 65],
                                  ptj[:, n * 512:(n + 1) * 512],
                                  start=(j == 0), stop=(j == 15))
                hsl = slice(qh * 1024, (qh + 1) * 1024)
                fw.recip(C.row_f32[0:1, hsl], pvh[64:65, :])
                fw.act(C.row_bf[0:1, hsl], C.row_f32[0:1, hsl], AF.Copy)
                fw.act(sv[hs, hsl], pvh[0:64, :], AF.Copy,
                       tag=f"{seq_tag}svev{pair}.{h}.{qh}")
                for n in range(2):
                    fw.mm(pvn[n], C.ones1[0:1, :],
                          C.row_bf[0:1, qh * 1024 + n * 512:qh * 1024 + (n + 1) * 512],
                          start=True, stop=True)
                fw.tt(sv[hs, hsl], sv[hs, hsl], pvh[0:64, :], ALU.mult,
                      tag=f"{seq_tag}svn{pair}.{h}.{qh}")

    for o in range(8):
        fw.phase = seq_tag + "l0.oproj"
        slot = C.wsQ if o % 2 == 0 else C.wsO
        _stream_w_chunk(fw, C, slot, W["woT"], o * 128, 128, tag=f"{seq_tag}wo{o}")

        def evac(h, ph, o=o):
            hsl = slice(h * 1024, (h + 1) * 1024)
            fw.stt(C.xT[o][:, hsl], ph[:, :], C.bias2[:, 24 + o:24 + o + 1],
                   C.xT[o][:, hsl], ALU.add, ALU.add, tag=f"{seq_tag}xadd{o}.{h}")
        _proj_1024(fw, C, slot, evac, C.svT)



def alloc_l0_moe(nc, stack, fw, C):
    sb = lambda name, shape, dt: stack.enter_context(nc.sbuf_tensor(name, shape, dt))
    P = C.pool
    C.guc = [[P[e * 4 + ic] for ic in range(4)] for e in range(2)]
    C.c_sb = [P[8], P[9]]
    C.sg = P[10]
    C.wg_ic = [P[11], P[12]]
    C.wu_ic = [P[13], P[14]]
    C.wd_sl = P[15]
    C.gwT = sb("gw_sb", [128, 256], BF16)
    C.gb_sb = sb("gb_sb", [128, 32], F32)
    C.scores_tm = sb("scores_tm", [128, 512], F32)
    C.sc_tm = sb("sc_tm", [128, 512], F32)
    C.comb_tm = sb("comb_tm", [128, 512], F32)
    P0f = P[0].bitcast(F32)
    P1f = P[1].bitcast(F32)
    P2f = P[2].bitcast(F32)
    C.r_msk = P0f[:, 0:512]
    C.r_m8 = P0f[:, 512:1024]
    C.r_eq = P1f[:, 0:512]
    C.r_num = P1f[:, 512:1024]
    C.r_gs = P2f[:, 0:128]
    C.r_t1 = P2f[:, 128:256]
    C.r_gm = P2f[:, 256:384]
    C.r_mx = P2f[:, 384:400]
    C.r_den = P2f[:, 400:416]
    C.r_h = (P0f, P1f, P2f)


def l0_moe_init(fw, C, W):
    src = W["gwT"].rearrange("(a p) n -> p a n", p=128)
    fw.dma(C.gwT.ap().rearrange("p (a n) -> p a n", n=32), src, eng="pool")
    fw.dma(C.gb_sb[:, :], W["gb128"], eng="sp")


def _routing(fw, C, seq_tag=""):
    fw.phase = seq_tag + "routing"
    """DeepSeek router: sigmoid scores, group top-2-sum, top-4 groups,
    top-8 experts, normalized+scaled combine. All token chunks batched:
    [128 tokens-per-chunk, 16 chunks, ...] on the free dim."""
    import concourse.bass as _bass
    for kc in range(8):
        lhsT = C.gwT[:, kc * 32:(kc + 1) * 32]
        for n in range(4):
            sl = slice(n * 512, (n + 1) * 512)
            fw.mm(C.An[n][0:32, :], lhsT, C.hT[kc][:, sl], start=(kc == 0),
                  stop=(kc == 7))
    for h in range(2):
        fw.act(C.scores_sb[0:32, h * 1024:(h + 1) * 1024], C.Ah[h][0:32, :],
               AF.Sigmoid, tag=f"{seq_tag}sig{h}")
    for tc in range(16):
        ts32 = slice(tc * 32, (tc + 1) * 32)
        bb = C.Bh[tc % 2]
        fw.trans(bb[:, 0:32], C.scores_sb[0:32, tc * 128:(tc + 1) * 128],
                 C.eye_f32[0:32, 0:32])
        if tc % 2 == 0:
            fw.copy(C.scores_tm[:, ts32], bb[:, 0:32], tag=f"{seq_tag}sctm{tc}")
        else:
            fw.act(C.scores_tm[:, ts32], bb[:, 0:32], AF.Copy,
                   tag=f"{seq_tag}sctm{tc}")
    def v3(t, e):
        return t.rearrange("p (t e) -> p t e", e=e)
    gbb = _bass.AP(C.gb_sb, 0, [[32, 128], [0, 16], [1, 32]])
    fw.eng_op("dve", lambda eng: eng.tensor_tensor(
        v3(C.scores_tm[:, :], 32), v3(C.scores_tm[:, :], 32), gbb, ALU.add),
        [C.scores_tm[:, :], C.gb_sb[:, :]], [C.sc_tm[:, :]], tag=f"{seq_tag}scb")
    fw.ops.pop()
    fw.eng_op("dve", lambda eng: eng.tensor_tensor(
        v3(C.sc_tm[:, :], 32), v3(C.scores_tm[:, :], 32), gbb, ALU.add),
        [C.scores_tm[:, :], C.gb_sb[:, :]], [C.sc_tm[:, :]], tag=f"{seq_tag}scb")
    sc4 = C.sc_tm[:, :].rearrange("p (t g e) -> p t g e", g=8, e=4)
    gs3 = v3(C.r_gs, 8)
    t13 = v3(C.r_t1, 8)
    first = True
    for i in range(4):
        for j in range(i + 1, 4):
            fw.tt(t13, sc4[:, :, :, i], sc4[:, :, :, j], ALU.add)
            if first:
                fw.copy(C.r_gs, C.r_t1)
                first = False
            else:
                fw.tt(gs3, gs3, t13, ALU.max)
    fw.memset(C.r_gm, 0.0)
    mxb8 = _bass.AP(C.r_mx.tensor, C.r_mx.offset, [[1024, 128], [1, 16], [0, 8]])
    gm3 = v3(C.r_gm, 8)
    eq3 = v3(C.r_eq[:, 0:128], 8)
    for it in range(4):
        fw.red(C.r_mx, gs3, ALU.max)
        fw.tt(eq3, gs3, mxb8, ALU.is_equal)
        fw.tt(gm3, gm3, eq3, ALU.add)
        fw.stt(C.r_gs, C.r_eq[:, 0:128], -2e9, C.r_gs, ALU.mult, ALU.add)
    msk4 = C.r_msk.rearrange("p (t g e) -> p t g e", g=8, e=4)
    gmb4 = _bass.AP(C.r_gm.tensor, C.r_gm.offset, [[1024, 128], [8, 16], [1, 8], [0, 4]])
    fw.eng_op("dve", lambda eng: eng.tensor_tensor(msk4, sc4, gmb4, ALU.mult),
              [C.sc_tm[:, :], C.r_gm], [C.r_msk], tag=f"{seq_tag}msk")
    fw.memset(C.r_m8, 0.0)
    msk3 = v3(C.r_msk, 32)
    m83 = v3(C.r_m8, 32)
    eq32 = v3(C.r_eq, 32)
    mxb32 = _bass.AP(C.r_mx.tensor, C.r_mx.offset, [[1024, 128], [1, 16], [0, 32]])
    for it in range(8):
        fw.red(C.r_mx, msk3, ALU.max)
        fw.tt(eq32, msk3, mxb32, ALU.is_equal)
        fw.tt(m83, m83, eq32, ALU.add)
        fw.stt(C.r_msk, C.r_eq, -2e9, C.r_msk, ALU.mult, ALU.add)
    fw.tt(C.r_num, C.r_m8, C.scores_tm[:, :], ALU.mult)
    fw.red(C.r_den, v3(C.r_num, 32), ALU.add)
    fw.ts(C.r_den, C.r_den, 1e-20, None, ALU.add)
    fw.recip(C.r_den, C.r_den)
    fw.ts(C.r_den, C.r_den, 2.5, None, ALU.mult)
    denb = _bass.AP(C.r_den.tensor, C.r_den.offset, [[1024, 128], [1, 16], [0, 32]])
    fw.eng_op("dve", lambda eng: eng.tensor_tensor(
        v3(C.comb_tm[:, :], 32), v3(C.r_num, 32), denb, ALU.mult),
        [C.r_num, C.r_den], [C.comb_tm[:, :]], tag=f"{seq_tag}comb")
    for tc in range(16):
        ts32 = slice(tc * 32, (tc + 1) * 32)
        bb = C.Bh[tc % 2]
        fw.trans(bb[0:32, 0:128], C.comb_tm[:, ts32], C.eye_f32[:, :])
        if tc % 2 == 0:
            fw.copy(C.combT[0:32, tc * 128:(tc + 1) * 128], bb[0:32, 0:128],
                    tag=f"{seq_tag}cT{tc}")
        else:
            fw.act(C.combT[0:32, tc * 128:(tc + 1) * 128], bb[0:32, 0:128],
                   AF.Copy, tag=f"{seq_tag}cT{tc}")


def l0_moe(fw, C, W, seq_tag=""):
    layernorm(fw, C, tag=seq_tag + "moe.")
    _routing(fw, C, seq_tag=seq_tag)

    def dma_gu(which, par, src, ic, tag):
        slot = (C.wg_ic if which == "g" else C.wu_ic)[par]
        dst = slot[:, (ic % 2) * 1024:(ic % 2 + 1) * 1024]
        sr = src.rearrange("(a p) n -> p a n", p=128)
        fw.dma(dst.rearrange("p (a n) -> p a n", n=128),
               sr[:, :, ic * 128:(ic + 1) * 128], eng="sp", tag=tag)

    for pair in range(17):
        shared = (pair == 16)
        es = (2 * pair, 2 * pair + 1) if not shared else (0, 1)
        srcs = []
        for par, e in enumerate(es):
            if shared:
                srcs.append((W["shg2"][e], W["shu2"][e], W["shd2"][e]))
            else:
                srcs.append((W["wgT"][e], W["wuT"][e], W["wdT"][e]))
        for par, e in enumerate(es):
            for ic in (0, 1):
                dma_gu("g", par, srcs[par][0], ic, f"{seq_tag}wg{e}.{ic}")
                dma_gu("u", par, srcs[par][1], ic, f"{seq_tag}wu{e}.{ic}")
        def emit_csb():
            if shared:
                return
            for par2, e2 in enumerate(es):
                fw.dma(C.row_f32, C.combT[e2:e2 + 1, :], eng="act",
                       tag=f"{seq_tag}crow{e2}")
                fw.act(C.row_bf, C.row_f32, AF.Copy)
                for n in range(4):
                    sl = slice(n * 512, (n + 1) * 512)
                    fw.mm(C.Bn[n], C.ones1[0:1, :], C.row_bf[0:1, sl],
                          start=True, stop=True)
                for h in range(2):
                    fw.act(_halves(C.c_sb[par2])[h], C.Bh[h][:, :], AF.Copy,
                           tag=f"{seq_tag}csb{e2}.{h}")
        for par, e in enumerate(es):
            fw.phase = seq_tag + "moe.gu"
            for ic in range(4):
                gslot = C.wg_ic[par][:, (ic % 2) * 1024:(ic % 2 + 1) * 1024]
                uslot = C.wu_ic[par][:, (ic % 2) * 1024:(ic % 2 + 1) * 1024]
                for kc in range(8):
                    lhsT = gslot[:, kc * 128:(kc + 1) * 128]
                    for n in range(4):
                        sl = slice(n * 512, (n + 1) * 512)
                        fw.mm(C.An[n], lhsT, C.hT[kc][:, sl], start=(kc == 0),
                              stop=(kc == 7))
                if ic < 2:
                    dma_gu("g", par, srcs[par][0], ic + 2, f"{seq_tag}wg{e}.{ic+2}")
                if par == 0 and ic == 0:
                    emit_csb()
                for h in range(2):
                    fw.act(_halves(C.sg)[h], C.Ah[h][:, :], AF.Silu,
                           tag=f"{seq_tag}silu{e}.{ic}.{h}")
                for kc in range(8):
                    lhsT = uslot[:, kc * 128:(kc + 1) * 128]
                    for n in range(4):
                        sl = slice(n * 512, (n + 1) * 512)
                        fw.mm(C.Bn[n], lhsT, C.hT[kc][:, sl], start=(kc == 0),
                              stop=(kc == 7))
                if ic < 2:
                    dma_gu("u", par, srcs[par][1], ic + 2, f"{seq_tag}wu{e}.{ic+2}")
                for h in range(2):
                    gh = _halves(C.guc[par][ic])[h]
                    guc_eng = "dve"
                    fw.act(gh, C.Bh[h][:, :], AF.Copy, tag=f"{seq_tag}uev{e}.{ic}.{h}")
                    fw.tt(gh, gh, _halves(C.sg)[h], ALU.mult, eng=guc_eng)
                    if not shared:
                        fw.tt(gh, gh, _halves(C.c_sb[par])[h], ALU.mult, eng=guc_eng,
                              tag=f"{seq_tag}guc{e}.{ic}.{h}")
        fw.phase = seq_tag + "moe.down"

        def dma_wd(o2):
            half_ = C.wd_sl[:, (o2 % 2) * 1024:(o2 % 2 + 1) * 1024]
            for par2, e2 in enumerate(es):
                dsr = srcs[par2][2][:, o2 * 128:(o2 + 1) * 128].rearrange(
                    "(a p) n -> p a n", p=128)
                fw.dma(half_[:, par2 * 512:(par2 + 1) * 512].rearrange(
                    "p (a n) -> p a n", n=128), dsr,
                    eng="sp", tag=f"{seq_tag}wd{e2}.{o2}")
        dma_wd(0)
        dma_wd(1)
        for o in range(8):
            half = C.wd_sl[:, (o % 2) * 1024:(o % 2 + 1) * 1024]
            Pn, Ph = (C.An, C.Ah) if o % 2 == 0 else (C.Bn, C.Bh)
            for par in range(2):
                for ic in range(4):
                    lhsT = half[:, par * 512 + ic * 128:par * 512 + (ic + 1) * 128]
                    for n in range(4):
                        sl = slice(n * 512, (n + 1) * 512)
                        fw.mm(Pn[n], lhsT, C.guc[par][ic][:, sl],
                              start=(par == 0 and ic == 0), stop=(par == 1 and ic == 3))
            if o < 6:
                dma_wd(o + 2)
            for h in range(2):
                hsl = slice(h * 1024, (h + 1) * 1024)
                fw.tt(C.xT[o][:, hsl], C.xT[o][:, hsl], Ph[h][:, :], ALU.add,
                      tag=f"{seq_tag}moex{o}.{h}")



def alloc_l1_attn(nc, stack, fw, C):
    sb = lambda name, shape, dt: stack.enter_context(nc.sbuf_tensor(name, shape, dt))
    P = C.pool
    C.n_k1 = [P[3][:, 0:1024], P[3][:, 1024:2048]]
    C.n_k3T = [P[15][:, 0:1024], P[15][:, 1024:2048]]
    C.n_k3 = P[16][0:64, :]
    C.n_k1T = P[17][0:64, :]
    C.n_lmf = [sb(f"n_lmf{i}", [128, 128], F32) for i in range(2)]
    C.n_lmb = [sb(f"n_lmb{i}", [128, 128], BF16) for i in range(2)]
    C.n_s = [sb(f"n_s{i}", [64, 11 * 64], BF16) for i in range(2)]
    C.n_cI = sb("n_cI", [64, 192], BF16)
    C.n_k1sum = [sb(f"n_k1sum{i}", [128, 16], F32) for i in range(2)]
    C.n_vsmall = [sb(f"n_vsmall{i}", [128, 4], F32) for i in range(2)]


def _sl64(t, i):
    return t[:, i * 64:(i + 1) * 64]


def l1_attn_init(fw, C):
    for i, v in enumerate((7.0, 15.0, 13.0)):
        fw.ts(_sl64(C.n_cI, i), C.eye_bf[0:64, 0:64], v, None, ALU.mult)


def l1_attn(fw, C, W, seq_tag=""):
    layernorm(fw, C, tag=seq_tag + "L1a.")

    for pair in range(8):
        fw.phase = seq_tag + "l1.qkv"
        for col0, slot in ((0, C.wsQ), (1024, C.wsK), (2048, C.wsV)):
            _stream_w_chunk(fw, C, slot, W["w_qkv"], col0 + pair * 128, 128,
                            tag=f"{seq_tag}l1w{pair}.{col0}")
        for (slot, dst, scale_ap) in ((C.wsQ, C.qb, cst(C, C.QSCALE)),
                                      (C.wsK, C.kb, None), (C.wsV, C.vb, None)):
            def evac(h, ph, dst=dst, scale_ap=scale_ap):
                if scale_ap is not None:
                    fw.act(_halves(dst)[h], ph[:, :], AF.Identity,
                           scale=scale_ap, tag=f"{seq_tag}l1ev{pair}.{h}")
                else:
                    fw.act(_halves(dst)[h], ph[:, :], AF.Copy,
                           tag=f"{seq_tag}l1ev{pair}.{h}")
            _proj_1024(fw, C, slot, evac, C.hT)

        fw.phase = seq_tag + "l1.lm"
        for h in (0, 1):
            hs = slice(64 * h, 64 * h + 64)
            fw.red(C.n_lmf[h][hs, 0:64],
                   C.kb[hs, :].rearrange("p (m w) -> p m w", w=32), ALU.add)
            fw.act(C.n_lmb[h][hs, 0:64], C.n_lmf[h][hs, 0:64], AF.Identity,
                   scale=cst(C, C.INV32)[hs, :])
            fw.red(C.n_lmf[h][hs, 64:128],
                   C.qb[hs, :].rearrange("p (m w) -> p m w", w=32), ALU.add)
            fw.act(C.n_lmb[h][hs, 64:128], C.n_lmf[h][hs, 64:128], AF.Identity,
                   scale=cst(C, C.INV32)[hs, :])
        for h in (0, 1):
            fw.phase = seq_tag + "l1.head"
            hs = slice(64 * h, 64 * h + 64)
            vtm = C.vtm[h]
            S = C.n_s[h]
            K2T, KVb, KVT, Aw, Cw, Ew, Vb, VT, M1b, M2b, K2b = range(11)
            ident = C.eye_bf[hs, hs]
            ident0 = C.eye_bf[0:64, 0:64]
            k1 = C.n_k1[h]
            k3 = C.n_k3
            k3T = C.n_k3T[h]
            k1T = C.n_k1T
            vsm = C.n_vsmall[h]
            _vtm_ones(fw, C, h)
            for kc in range(16):
                _tc(fw, C, vtm[:, kc * 65:kc * 65 + 64],
                    C.vb[hs, kc * 128:(kc + 1) * 128], ident, kc, 64, 128,
                    tag=f"{seq_tag}nvtm")
            klm = C.n_lmb[h][hs, 0:64]
            qlm = C.n_lmb[h][hs, 64:128]
            for tc in range(16):
                ah = C.Ah[tc // 8]
                ofs = (tc % 8) * 128
                fw.mm(ah[:, ofs:ofs + 64], C.qb[hs, tc * 128:(tc + 1) * 128],
                      klm, start=True, stop=True)
            for tc in range(16):
                ah = C.Ah[tc // 8]
                ofs = (tc % 8) * 128
                fw.act(k1[:, tc * 64:(tc + 1) * 64], ah[:, ofs:ofs + 64],
                       AF.Exp, accum_out=C.n_k1sum[h][:, tc:tc + 1],
                       tag=f"{seq_tag}k1e{tc}")
            fw.recip(C.n_k1sum[h][:, :], C.n_k1sum[h][:, :])
            for tc in range(16):
                fw.ts(k1[:, tc * 64:(tc + 1) * 64], k1[:, tc * 64:(tc + 1) * 64],
                      C.n_k1sum[h][:, tc:tc + 1], None, ALU.mult)
            for tc in range(16):
                _tc(fw, C, k1T[0:64, tc * 128:(tc + 1) * 128],
                    k1[:, tc * 64:(tc + 1) * 64], C.eye_bf[:, :], tc, 128, 64,
                    tag=f"{seq_tag}k1T{tc}")
            fw.mm(C.BL[0:64, 0:64], qlm, klm, start=True, stop=True)
            fw.act(_sl64(S, K2b)[0:64, :], C.BL[0:64, 0:64], AF.Exp,
                   accum_out=vsm[0:64, 1:2])
            fw.recip(vsm[0:64, 1:2], vsm[0:64, 1:2])
            fw.ts(_sl64(S, K2b)[0:64, :], _sl64(S, K2b)[0:64, :],
                  vsm[0:64, 1:2], None, ALU.mult)
            for n in range(4):
                sl = slice(n * 512, (n + 1) * 512)
                fw.mm(C.Bn[n][0:64, :], qlm, C.kb[hs, sl], start=True, stop=True)
            for hh in range(2):
                fw.act(k3[0:64, hh * 1024:(hh + 1) * 1024], C.Bh[hh][0:64, :], AF.Exp,
                       accum_out=vsm[0:64, 2 + hh:3 + hh], tag=f"{seq_tag}k3e{hh}")
            fw.tt(vsm[0:64, 0:1], vsm[0:64, 2:3], vsm[0:64, 3:4], ALU.add)
            fw.recip(vsm[0:64, 0:1], vsm[0:64, 0:1])
            fw.ts(k3[0:64, :], k3[0:64, :], vsm[0:64, 0:1], None, ALU.mult)
            for tc in range(16):
                _tc(fw, C, k3T[:, tc * 64:(tc + 1) * 64],
                    k3[0:64, tc * 128:(tc + 1) * 128], ident0, tc, 64, 128,
                    tag=f"{seq_tag}k3T{tc}")
            fw.phase = seq_tag + "l1.newton"
            fw.trans(C.PTbB[0:64, 64:128], _sl64(S, K2b)[0:64, :], ident0)
            fw.copy(_sl64(S, K2T)[0:64, :], C.PTbB[0:64, 64:128])
            fw.mm(C.BL[0:1, 0:64], C.allones[0:64, 0:1], _sl64(S, K2b)[0:64, :],
                  start=True, stop=True)
            fw.red(vsm[0:1, 2:3], C.BL[0:1, 0:64], ALU.max)
            fw.recip(vsm[0:1, 2:3], vsm[0:1, 2:3])
            fw.act(C.row_bf[0:1, 0:1], vsm[0:1, 2:3], AF.Copy)
            fw.mm(C.BL[0:64, 64:65], C.ones1[0:1, 0:64], C.row_bf[0:1, 0:1],
                  start=True, stop=True)
            fw.copy(vsm[0:64, 3:4], C.BL[0:64, 64:65])
            fw.ts(_sl64(S, Vb)[0:64, :], _sl64(S, K2T)[0:64, :],
                  vsm[0:64, 3:4], None, ALU.mult)
            for it in range(6):
                fw.mm(C.BL[0:64, 0:64], _sl64(S, K2T)[0:64, :], _sl64(S, Vb)[0:64, :],
                      start=True, stop=True)
                fw.copy(_sl64(S, KVb)[0:64, :], C.BL[0:64, 0:64])
                fw.tt(_sl64(S, Aw)[0:64, :], _sl64(C.n_cI, 0)[0:64, :],
                      C.BL[0:64, 0:64], ALU.subtract)
                fw.trans(C.PTbB[0:64, 256:320], _sl64(S, KVb)[0:64, :], ident0)
                fw.copy(_sl64(S, KVT)[0:64, :], C.PTbB[0:64, 256:320])
                fw.mm(C.BL[0:64, 0:64], _sl64(S, KVT)[0:64, :], _sl64(S, Aw)[0:64, :],
                      start=True, stop=True)
                fw.tt(_sl64(S, Cw)[0:64, :], _sl64(C.n_cI, 1)[0:64, :],
                      C.BL[0:64, 0:64], ALU.subtract)
                fw.mm(C.BL[0:64, 0:64], _sl64(S, KVT)[0:64, :], _sl64(S, Cw)[0:64, :],
                      start=True, stop=True)
                fw.tt(_sl64(S, Ew)[0:64, :], _sl64(C.n_cI, 2)[0:64, :],
                      C.BL[0:64, 0:64], ALU.subtract)
                fw.trans(C.PTbB[0:64, 256:320], _sl64(S, Vb)[0:64, :], ident0)
                fw.copy(_sl64(S, VT)[0:64, :], C.PTbB[0:64, 256:320])
                fw.mm(C.BL[0:64, 0:64], _sl64(S, VT)[0:64, :], _sl64(S, Ew)[0:64, :],
                      start=True, stop=True)
                fw.act(_sl64(S, Vb)[0:64, :], C.BL[0:64, 0:64], AF.Identity,
                       scale=cst(C, 6)[0:64, :], tag=f"{seq_tag}nwt{it}")
            fw.phase = seq_tag + "l1.M"
            for kc in range(16):
                fw.mm(C.BL[0:64, 64:128], k3T[:, kc * 64:(kc + 1) * 64],
                      vtm[:, kc * 65:kc * 65 + 64], start=(kc == 0), stop=(kc == 15))
            fw.copy(_sl64(S, M1b)[0:64, :], C.BL[0:64, 64:128])
            fw.trans(C.PTbB[0:64, 256:320], _sl64(S, Vb)[0:64, :], ident0)
            fw.copy(_sl64(S, VT)[0:64, :], C.PTbB[0:64, 256:320])
            fw.mm(C.BL[0:64, 0:64], _sl64(S, VT)[0:64, :], _sl64(S, M1b)[0:64, :],
                  start=True, stop=True)
            fw.copy(_sl64(S, M2b)[0:64, :], C.BL[0:64, 0:64])
            for n in range(4):
                sl = slice(n * 512, (n + 1) * 512)
                fw.mm(C.Bn[n][0:64, :], _sl64(S, M2b)[0:64, :], k1T[0:64, sl],
                      start=True, stop=True)
            for hh in range(2):
                fw.act(C.svT[pair][hs, hh * 1024:(hh + 1) * 1024], C.Bh[hh][0:64, :],
                       AF.Copy, tag=f"{seq_tag}svl1{pair}.{h}.{hh}")

    for o in range(8):
        fw.phase = seq_tag + "l1.oproj"
        slot = C.wsQ if o % 2 == 0 else C.wsO
        _stream_w_chunk(fw, C, slot, W["w_proj"], o * 128, 128, tag=f"{seq_tag}l1wo{o}")

        def evac(h, ph, o=o):
            hsl = slice(h * 1024, (h + 1) * 1024)
            fw.tt(C.xT[o][:, hsl], C.xT[o][:, hsl], ph[:, :], ALU.add,
                  tag=f"{seq_tag}l1x{o}.{h}")
        _proj_1024(fw, C, slot, evac, C.svT)



def alloc_l1_ffn(nc, stack, fw, C):
    P = C.pool
    C.f_dec = [P[4], P[5], P[6], P[7]]
    C.f_acts = [P[8], P[9], P[10], P[11]]
    C.f_mT = [P[12], P[17], P[0], P[1]]
    C.f_ind = P[13]
    C.f_win = [P[14], P[15]]
    C.f_wout = P[16]
    C.f_binb = C.mean.bitcast(BF16)[0:1, 0:2040]
    C.f_scr = C.var


def l1_ffn(fw, C, W, seq_tag=""):
    layernorm(fw, C, tag=seq_tag + "ffn.")
    fw.dma(C.f_binb, W["b_in"], eng="pool", tag=f"{seq_tag}binb")
    P4 = (C.AL, C.AH, C.BL, C.BH)

    for tg in range(4):
        fw.phase = seq_tag + "ffn"
        tcs = tuple(4 * tg + j for j in range(4))
        for wh in range(2):
            w0 = wh * 1024
            wn = 1024 if wh == 0 else 1016
            for kc in range(8):
                slot = C.f_win[kc % 2]
                fw.dma(slot[:, 0:wn], W["w_inT"][kc * 128:(kc + 1) * 128, w0:w0 + wn],
                       eng="sp", tag=f"{seq_tag}win{tg}.{wh}.{kc}")
                for j, tc in enumerate(tcs):
                    lhsT = C.hT[kc][:, tc * 128:(tc + 1) * 128]
                    for (a, b) in ((0, 512), (512, wn)):
                        if kc == 0:
                            fw.mm(P4[j][:, a:b], C.ones1[0:1, :],
                                  C.f_binb[0:1, w0 + a:w0 + b], start=True, stop=False)
                        fw.mm(P4[j][:, a:b], lhsT, slot[:, a:b],
                              start=False, stop=(kc == 7))
            for j, tc in enumerate(tcs):
                fw.ts(C.f_dec[j][:, w0:w0 + wn], P4[j][:, 0:wn], 0.0, None, ALU.is_gt)
                fw.act(C.f_acts[j][:, w0:w0 + wn], P4[j][:, 0:wn], AF.Silu,
                       tag=f"{seq_tag}fsilu{tc}.{wh}")
        for j, tc in enumerate(tcs):
            ind3 = C.f_ind[:, 0:2040].rearrange("p (r n) -> p r n", n=255)
            dec3 = C.f_dec[j][:, 0:2040].rearrange("p (r n) -> p r n", n=255)
            fw.memset(ind3[:, :, 0:1], 1.0)
            for d in range(7):
                p0 = 2 ** d - 1
                n = 2 ** d
                c0 = 2 ** (d + 1) - 1
                par = ind3[:, :, p0:p0 + n]
                dd = dec3[:, :, p0:p0 + n]
                right = ind3[:, :, c0 + 1:c0 + 2 * n:2]
                left = ind3[:, :, c0:c0 + 2 * n:2]
                fw.tt(right, par, dd, ALU.mult)
                fw.tt(left, par, right, ALU.subtract)
            fw.tt(C.f_acts[j][:, 0:2040], C.f_acts[j][:, 0:2040],
                  C.f_ind[:, 0:2040], ALU.mult, tag=f"{seq_tag}fmask{tc}")
            for kc in range(16):
                wn2 = 2040 - kc * 128 if kc == 15 else 128
                _tc(fw, C, C.f_mT[j][0:wn2, kc * 128:(kc + 1) * 128],
                    C.f_acts[j][:, kc * 128:kc * 128 + wn2], C.eye_bf[:, :],
                    kc + j, 128, wn2, tag=f"{seq_tag}fmT{tc}.{kc}")
        for kc in range(16):
            wn2 = 2040 - kc * 128 if kc == 15 else 128
            half = C.f_wout[:, (kc % 2) * 1024:(kc % 2 + 1) * 1024]
            fw.dma(half[0:wn2, :], W["w_outT"][kc * 128:kc * 128 + wn2, :],
                   eng="sp", tag=f"{seq_tag}wo{tg}.{kc}")
            for j, tc in enumerate(tcs):
                lhsT = C.f_mT[j][0:wn2, kc * 128:(kc + 1) * 128]
                for n in range(2):
                    fw.mm(P4[j][:, n * 512:(n + 1) * 512], lhsT,
                          half[0:wn2, n * 512:(n + 1) * 512],
                          start=(kc == 0), stop=(kc == 15))
        for j, tc in enumerate(tcs):
            fw.act(C.f_scr[:, 0:1024], P4[j][:, :], AF.Copy, tag=f"{seq_tag}fev{tc}")
            for o in range(8):
                bb = P4[j]
                fw.trans(bb[:, 0:128], C.f_scr[:, o * 128:(o + 1) * 128],
                         C.eye_f32[:, :])
                fw.tt(C.xT[o][:, tc * 128:(tc + 1) * 128],
                      C.xT[o][:, tc * 128:(tc + 1) * 128], bb[:, 0:128], ALU.add,
                      tag=f"{seq_tag}fx{tc}.{o}")



NCORES = 8
NSEQ = 4


def _host_consts():
    return {
        "c_allones": np.ones((128, 128), np.float32),
        "c_ones1": np.ones((1, 128), np.float32),
        "c_eye": np.eye(128, dtype=np.float32),
        "c_cst": np.broadcast_to(
            np.array([1e-5, -0.5, 0.125, 2.5, 1e-20, 1.0 / 32, 0.25, 0], np.float32),
            (128, 8)).copy(),
    }


CONST_SHAPES = {"c_allones": [128, 128], "c_ones1": [1, 128],
                "c_eye": [128, 128], "c_cst": [128, 8]}

L0A_SHAPES = {"wqT": [1024, 1024], "wkT": [1024, 1024], "wvT": [1024, 1024],
              "woT": [1024, 1024], "b_l0": [128, 32]}
MOE_SHAPES = {"gwT": [1024, 32], "gb128": [128, 32],
              "wgT": [32, 1024, 512], "wuT": [32, 1024, 512], "wdT": [32, 512, 1024],
              "shg2": [2, 1024, 512], "shu2": [2, 1024, 512], "shd2": [2, 512, 1024]}
L1A_SHAPES = {"w_qkv": [1024, 3072], "w_proj": [1024, 1024]}
L1F_SHAPES = {"w_inT": [1024, 2040], "b_in": [1, 2040], "w_outT": [2040, 1024]}


def _prep_l0a(W_qkv, b_qkv, W_out, b_out):
    qr = np.concatenate([np.arange(h * 192, h * 192 + 64) for h in range(16)])
    out = {
        "wqT": np.ascontiguousarray(W_qkv[qr, :].T),
        "wkT": np.ascontiguousarray(W_qkv[qr + 64, :].T),
        "wvT": np.ascontiguousarray(W_qkv[qr + 128, :].T),
        "woT": np.ascontiguousarray(W_out.T),
    }
    bq = b_qkv[qr].reshape(8, 128).T
    bk = b_qkv[qr + 64].reshape(8, 128).T
    bv = b_qkv[qr + 128].reshape(8, 128).T
    bo = b_out.reshape(8, 128).T
    out["b_l0"] = np.ascontiguousarray(
        np.concatenate([bq, bk, bv, bo], axis=1)).astype(np.float32)
    return out


def _prep_moe(gate_w, gate_bias, w_gate, w_up, w_down, sh_gate, sh_up, sh_down):
    shgT, shuT, shdT = sh_gate.T, sh_up.T, sh_down.T
    return {
        "gwT": np.ascontiguousarray(gate_w.T),
        "gb128": np.broadcast_to(gate_bias[None, :], (128, 32)).astype(np.float32).copy(),
        "wgT": np.ascontiguousarray(w_gate.transpose(0, 2, 1)),
        "wuT": np.ascontiguousarray(w_up.transpose(0, 2, 1)),
        "wdT": np.ascontiguousarray(w_down.transpose(0, 2, 1)),
        "shg2": np.ascontiguousarray(np.stack([shgT[:, 0:512], shgT[:, 512:1024]])),
        "shu2": np.ascontiguousarray(np.stack([shuT[:, 0:512], shuT[:, 512:1024]])),
        "shd2": np.ascontiguousarray(np.stack([shdT[0:512, :], shdT[512:1024, :]])),
    }


BF16_INPUTS = {"wqT", "wkT", "wvT", "woT", "wgT", "wuT", "wdT",
               "shg2", "shu2", "shd2", "w_qkv", "w_proj", "w_inT", "w_outT"}


def _declare(nc, shapes):
    d = {}
    for n, shp in shapes.items():
        dt = BF16 if n in BF16_INPUTS else F32
        d[n] = nc.dram_tensor(n, list(shp), dt, kind="ExternalInput").ap()
    return d


def _build_block(block):
    nc = bass.Bass("TRN2", target_bir_lowering=False)
    x_d = nc.dram_tensor("xT", [NSEQ, 1024, 2048], F32, kind="ExternalInput").ap()
    y_d = nc.dram_tensor("yT", [NSEQ, 1024, 2048], F32, kind="ExternalOutput").ap()
    dram = _declare(nc, CONST_SHAPES)
    wshapes = {"l0a": L0A_SHAPES, "moe": MOE_SHAPES,
               "l1a": L1A_SHAPES, "l1f": L1F_SHAPES}[block]
    W = _declare(nc, wshapes)

    import contextlib
    with contextlib.ExitStack() as stack:
        fw = FW(nc)
        C = alloc_common(nc, stack, fw)
        if block == "l0a":
            alloc_l0_attn(nc, stack, fw, C)
        elif block == "moe":
            alloc_l0_moe(nc, stack, fw, C)
        elif block == "l1a":
            alloc_l0_attn(nc, stack, fw, C)
            alloc_l1_attn(nc, stack, fw, C)
        else:
            alloc_l1_ffn(nc, stack, fw, C)
        emit_const_init(fw, C, dram)
        if block == "l0a":
            l0_attn_init(fw, C, W)
        elif block == "moe":
            l0_moe_init(fw, C, W)
        elif block == "l1a":
            for i in range(2):
                ap = C.vtm[i].ap().rearrange("p (a n) -> p a n", n=65)
                fw.memset(ap[:, :, 64:65], 1.0)
            l1_attn_init(fw, C)
        for s in range(NSEQ):
            load_xT(fw, C, x_d, s)
            if block == "l0a":
                l0_attn(fw, C, W, seq_tag=f"s{s}.")
            elif block == "moe":
                l0_moe(fw, C, W, seq_tag=f"s{s}.")
            elif block == "l1a":
                l1_attn(fw, C, W, seq_tag=f"s{s}.")
            else:
                l1_ffn(fw, C, W, seq_tag=f"s{s}.")
            store_xT(fw, C, y_d, s)
        fw.finalize(stack)
        fw.emit()
    return nc


def _build_fused():
    nc = bass.Bass("TRN2", target_bir_lowering=False)
    x_d = nc.dram_tensor("xT", [NSEQ, 1024, 2048], F32, kind="ExternalInput").ap()
    y_d = nc.dram_tensor("yT", [NSEQ, 1024, 2048], F32, kind="ExternalOutput").ap()
    dram = _declare(nc, CONST_SHAPES)
    W = {}
    for shapes in (L0A_SHAPES, MOE_SHAPES, L1A_SHAPES, L1F_SHAPES):
        W.update(_declare(nc, shapes))

    import contextlib
    with contextlib.ExitStack() as stack:
        fw = FW(nc)
        C = alloc_common(nc, stack, fw)
        alloc_l0_attn(nc, stack, fw, C)
        alloc_l0_moe(nc, stack, fw, C)
        alloc_l1_attn(nc, stack, fw, C)
        alloc_l1_ffn(nc, stack, fw, C)
        emit_const_init(fw, C, dram)
        l0_attn_init(fw, C, W)
        l0_moe_init(fw, C, W)
        l1_attn_init(fw, C)
        for s in range(NSEQ):
            load_xT(fw, C, x_d, s)
            l0_attn(fw, C, W, seq_tag=f"s{s}a.")
            l0_moe(fw, C, W, seq_tag=f"s{s}m.")
            l1_attn(fw, C, W, seq_tag=f"s{s}n.")
            l1_ffn(fw, C, W, seq_tag=f"s{s}f.")
            store_xT(fw, C, y_d, s)
        fw.finalize(stack)
        fw.emit()
    _flip_ldweights(nc, fw.flip_names)
    return nc


def _flip_ldweights(nc, names):
    import copy as _c
    todo = set(names)
    if not todo:
        return
    for blk in nc.main_func.blocks:
        il = blk.instructions
        hit = False
        for i, ins in enumerate(il):
            if ins.name in todo:
                il[i] = _c.replace(ins, ldweights=False)
                todo.discard(ins.name)
                hit = True
        if hit and not todo:
            break
    assert not todo, f"{len(todo)} skip-ldweights matmuls not found in BIR"


def kernel(L0_attn_x, L0_attn_W_qkv, L0_attn_b_qkv, L0_attn_W_out, L0_attn_b_out,
           L0_ffn_gate_w, L0_ffn_gate_bias, L0_ffn_w_gate, L0_ffn_w_up,
           L0_ffn_w_down, L0_ffn_sh_gate, L0_ffn_sh_up, L0_ffn_sh_down,
           L1_attn_w_qkv, L1_attn_w_proj, L1_ffn_W_in, L1_ffn_b_in, L1_ffn_W_out):
    f32 = lambda a: np.asarray(a, dtype=np.float32)
    x = f32(L0_attn_x)
    xT = np.ascontiguousarray(x.transpose(0, 2, 1))

    consts = _host_consts()
    w_l0a = _prep_l0a(f32(L0_attn_W_qkv), f32(L0_attn_b_qkv),
                      f32(L0_attn_W_out), f32(L0_attn_b_out))
    w_moe = _prep_moe(f32(L0_ffn_gate_w), f32(L0_ffn_gate_bias), f32(L0_ffn_w_gate),
                      f32(L0_ffn_w_up), f32(L0_ffn_w_down), f32(L0_ffn_sh_gate),
                      f32(L0_ffn_sh_up), f32(L0_ffn_sh_down))
    w_l1a = {"w_qkv": f32(L1_attn_w_qkv), "w_proj": f32(L1_attn_w_proj)}
    w_l1f = {"w_inT": np.ascontiguousarray(f32(L1_ffn_W_in).T),
             "b_in": f32(L1_ffn_b_in).reshape(1, 2040),
             "w_outT": np.ascontiguousarray(f32(L1_ffn_W_out).T)}

    shards = [np.ascontiguousarray(xT[i * NSEQ:(i + 1) * NSEQ]) for i in range(NCORES)]
    nc = _build_fused()
    import ml_dtypes
    wall = {}
    for d in (consts, w_l0a, w_moe, w_l1a, w_l1f):
        for n, a in d.items():
            wall[n] = (np.ascontiguousarray(a).astype(ml_dtypes.bfloat16)
                       if n in BF16_INPUTS else a)
    in_maps = []
    for i in range(NCORES):
        m = {"xT": shards[i]}
        m.update(wall)
        in_maps.append(m)
    res = run_bass_kernel_spmd(nc, in_maps, core_ids=list(range(NCORES)))
    shards = [np.asarray(r["yT"]) for r in res.results]

    yT = np.concatenate(shards, axis=0)
    return np.ascontiguousarray(yT.transpose(0, 2, 1))

